# version 09
# baseline (speedup 1.0000x reference)
import numpy as np
import ml_dtypes
from contextlib import ExitStack

import concourse.bass as bass
import concourse.mybir as mybir
from concourse.bass_utils import run_bass_kernel_spmd

F32 = mybir.dt.float32
BF16 = mybir.dt.bfloat16
AF = mybir.ActivationFunctionType
ALU = mybir.AluOpType
AX = mybir.AxisListType

D = 1024
TL = 2048
NXT = 16
H = 16
LAT = 64
QL = 512
TT = 256
NTT = TL // TT
NCH = 64
NP = H // 2
PPT = NCH * NP
EPS = 1e-5
NCORES = 8

_NC_CACHE = {}


def _apply_ldw_reuse(nc, names):
    names = set(names)
    for f in nc.m.functions:
        for bb in f.blocks:
            il = bb.instructions
            changed = False
            for k, ins in enumerate(il):
                if type(ins).__name__ == "InstMatmult" and ins.name in names:
                    clone = mybir.InstMatmult(
                        name=ins.name, engine=ins.engine,
                        ins=list(ins.ins), outs=list(ins.outs),
                        replication_resolution=ins.replication_resolution,
                        replication_shift_amnt=ins.replication_shift_amnt,
                        replication_num_rows=ins.replication_num_rows,
                        start_tensor_calc=ins.start_tensor_calc,
                        stop_tensor_calc=ins.stop_tensor_calc,
                        is_transpose=ins.is_transpose,
                        tile_position=ins.tile_position,
                        tile_size=ins.tile_size,
                        perf_mode=ins.perf_mode,
                        bass_skip_group_check=ins.bass_skip_group_check,
                        ldweights=False,
                    )
                    si = ins.sync_info
                    if si is not None:
                        clone.sync_info = si
                    il[k] = clone
                    changed = True
            if changed:
                bb.instructions = il


def _bf(x):
    return np.asarray(x).astype(ml_dtypes.bfloat16)


def _host_consts():
    if "consts" in _NC_CACHE:
        return _NC_CACHE["consts"]
    n = np.arange(D, dtype=np.float64)
    ang = 2.0 * np.pi * np.outer(n, n) / D
    C = np.cos(ang)
    S = np.sin(ang)

    def split(v):
        hi = _bf(v)
        lo = _bf(v - hi.astype(np.float64))
        return hi, lo

    C_hi, C_lo = split(C)
    S_hi, S_lo = split(S)
    Sn_hi, Sn_lo = -S_hi, -S_lo
    cs1 = []
    for r in range(NCORES):
        sl = slice(r * 128, (r + 1) * 128)
        cs1.append(np.ascontiguousarray(np.stack(
            [C_hi[:, sl], C_lo[:, sl], S_hi[:, sl], S_lo[:, sl],
             Sn_hi[:, sl], Sn_lo[:, sl]], axis=1)))
    cs2 = np.ascontiguousarray(np.stack([C_hi, C_lo, Sn_hi, Sn_lo], axis=0))
    ident = np.eye(128, dtype=np.float32)
    identb = _bf(np.eye(128, dtype=np.float32))
    out = (cs1, cs2, ident, identb)
    _NC_CACHE["consts"] = out
    return out


class Ctr:
    def __init__(self, nc, ctx, name):
        self.sem = ctx.enter_context(nc.semaphore(name))
        self.n = 0

    def inc(self, inst, k=1):
        inst.then_inc(self.sem, k)
        self.n += k
        return self.n

    def dma(self, inst):
        return self.inc(inst, 16)


def build_nc(stage=3):
    nc = bass.Bass("TRN2", target_bir_lowering=False)
    RMM = []

    def noldw(i):
        RMM.append(i.ins.name)
        return i
    dt = nc.dram_tensor
    xs = dt("xs", [TL, D], F32, kind="ExternalInput").ap()
    wqdT = dt("wqdT", [D, QL], F32, kind="ExternalInput").ap()
    wqkT = dt("wqkT", [QL, D], F32, kind="ExternalInput").ap()
    wkvdT = dt("wkvdT", [D, LAT], F32, kind="ExternalInput").ap()
    wvuT2 = dt("wvuT2", [128, D], F32, kind="ExternalInput").ap()
    wprojT = dt("wprojT", [D, D], F32, kind="ExternalInput").ap()
    wfr = dt("wfr", [D, D], F32, kind="ExternalInput").ap()
    wfi = dt("wfi", [D, D], F32, kind="ExternalInput").ap()
    cs1 = dt("cs1", [D, 6, 128], BF16, kind="ExternalInput").ap()
    cs2 = dt("cs2", [4, D, D], BF16, kind="ExternalInput").ap()
    bstack = dt("bstack", [21, 128], F32, kind="ExternalInput").ap()
    bprow = dt("bprow", [1, D], F32, kind="ExternalInput").ap()
    fbrow = dt("fbrow", [1, D], F32, kind="ExternalInput").ap()
    bvubc_d = dt("bvubc", [128, 1024], F32, kind="ExternalInput").ap()
    identd = dt("identd", [128, 128], F32, kind="ExternalInput").ap()
    identbd = dt("identbd", [128, 128], BF16, kind="ExternalInput").ap()
    out = dt("out", [TL, D], F32, kind="ExternalOutput").ap()

    cc_m_in = dt("cc_m_in", [256, D], BF16).ap()
    cc_m_out = dt("cc_m_out", [2048, D], BF16, addr_space="Shared").ap()
    cc_kv_in = dt("cc_kv_in", [LAT, TL], BF16).ap()
    cc_kv_out = dt("cc_kv_out", [4 * LAT, TL], BF16).ap()
    rs_dram = dt("rs_dram", [16, 128], F32).ap()

    xs_t = xs.rearrange("(tt p) d -> p tt d", p=128)
    out_t = out.rearrange("(tt p) d -> p tt d", p=128)

    with ExitStack() as CTX:
        d_in = Ctr(nc, CTX, "d_in")
        d_g = Ctr(nc, CTX, "d_g")
        ccs = Ctr(nc, CTX, "ccs")
        d_out = Ctr(nc, CTX, "d_out")
        tb = Ctr(nc, CTX, "tb")
        vb = Ctr(nc, CTX, "vb")
        sb = Ctr(nc, CTX, "sb")
        fS = Ctr(nc, CTX, "fS")
        fT = Ctr(nc, CTX, "fT")
        fV = Ctr(nc, CTX, "fV")
        pT = Ctr(nc, CTX, "pT")
        pS = Ctr(nc, CTX, "pS")
        pV = Ctr(nc, CTX, "pV")
        aS = Ctr(nc, CTX, "aS")
        aE = Ctr(nc, CTX, "aE")
        aZ = Ctr(nc, CTX, "aZ")
        aZ3 = Ctr(nc, CTX, "aZ3")
        mT = Ctr(nc, CTX, "mT")
        mV = Ctr(nc, CTX, "mV")

        sx = CTX.enter_context(nc.sbuf_tensor("sx", [128, NXT * D], F32))
        s_stats = CTX.enter_context(nc.sbuf_tensor("s_stats", [128, 64], F32))
        s_bcols = CTX.enter_context(nc.sbuf_tensor("s_bcols", [128, 24], F32))
        s_bprojbc = CTX.enter_context(nc.sbuf_tensor("s_bprojbc", [128, D], F32))
        s_fbhbc = CTX.enter_context(nc.sbuf_tensor("s_fbhbc", [128, D], F32))
        s_ones32 = CTX.enter_context(nc.sbuf_tensor("s_ones32", [128, 32], BF16))
        s_onesr = CTX.enter_context(nc.sbuf_tensor("s_onesr", [1, 128], F32))
        s_id = CTX.enter_context(nc.sbuf_tensor("s_id", [128, 128], F32))
        s_idb = CTX.enter_context(nc.sbuf_tensor("s_idb", [128, 128], BF16))
        s_xlnf = CTX.enter_context(nc.sbuf_tensor("s_xlnf", [128, 2 * D], F32))
        s_sqdump = CTX.enter_context(nc.sbuf_tensor("s_sqdump", [128, D], BF16))

        bctx = ExitStack()
        s_brow = bctx.enter_context(nc.sbuf_tensor("s_brow", [1, 2 * D], F32))
        s_bst = bctx.enter_context(nc.sbuf_tensor("s_bst", [21, 128], F32))

        block = CTX.enter_context(nc.Block())

        @block.sync
        def _(sync):
            sxv = sx[:, :].rearrange("p (tt d) -> p tt d", d=D)
            for q in range(4):
                i = sync.dma_start(out=sxv[:, q * 4:(q + 1) * 4, :],
                                   in_=xs_t[:, q * 4:(q + 1) * 4, :])
                d_in.dma(i)
            d_in.dma(sync.dma_start(out=s_id[:, :], in_=identd))
            d_in.dma(sync.dma_start(out=s_idb[:, :], in_=identbd))
            d_in.dma(sync.dma_start(out=s_bst[:, :], in_=bstack))
            d_in.dma(sync.dma_start(out=s_brow[0:1, 0:D], in_=bprow))
            d_in.dma(sync.dma_start(out=s_brow[0:1, D:2 * D], in_=fbrow))
        DX = [16 * (q + 1) for q in range(4)]
        D_SETUP = d_in.n

        @block.gpsimd
        def _(g):
            g.memset(s_ones32[:, :], 1.0)
            i = g.memset(s_onesr[:, :], 1.0)
            d_g.inc(i, 1)
        G_SETUP = d_g.n

        s_w32 = bctx.enter_context(nc.sbuf_tensor("s_w32", [128, 2 * D], F32))
        s_wrh = bctx.enter_context(nc.sbuf_tensor("s_wrh", [128, 8 * D], BF16))
        s_wrl = bctx.enter_context(nc.sbuf_tensor("s_wrl", [128, 8 * D], BF16))
        s_wih = bctx.enter_context(nc.sbuf_tensor("s_wih", [128, 8 * D], BF16))
        s_wil = bctx.enter_context(nc.sbuf_tensor("s_wil", [128, 8 * D], BF16))
        s_cs1 = bctx.enter_context(nc.sbuf_tensor("s_cs1", [128, 8 * 6 * 128], BF16))
        s_abh = bctx.enter_context(nc.sbuf_tensor("s_abh", [128, 2048], BF16))
        s_abl = bctx.enter_context(nc.sbuf_tensor("s_abl", [128, 2048], BF16))
        s_c2 = bctx.enter_context(nc.sbuf_tensor("s_c2", [128, 2 * 4 * D], BF16))
        s_msl = bctx.enter_context(nc.sbuf_tensor("s_msl", [128, 2 * D], BF16))
        pB = bctx.enter_context(nc.psum_tensor("pB", [128, 4096], F32))

        @block.gpsimd
        def _(g):
            for wi, wsrc in enumerate((wfr, wfi)):
                for ch in range(8):
                    k = wi * 8 + ch
                    if k >= 2:
                        g.wait_ge(vb.sem, 2 * (k - 1))
                    i = g.dma_start(out=s_w32[:, (k % 2) * D:(k % 2 + 1) * D],
                                    in_=wsrc[ch * 128:(ch + 1) * 128, :])
                    d_g.dma(i)

        @block.vector
        def _(v):
            for wi in range(2):
                th, tl = (s_wrh, s_wrl) if wi == 0 else (s_wih, s_wil)
                for ch in range(8):
                    k = wi * 8 + ch
                    v.wait_ge(d_g.sem, G_SETUP + 16 * (k + 1))
                    src = s_w32[:, (k % 2) * D:(k % 2 + 1) * D]
                    i = v.tensor_copy(th[:, ch * D:(ch + 1) * D], src)
                    vb.inc(i, 1)
                    v.wait_ge(vb.sem, vb.n)
                    i = v.scalar_tensor_tensor(tl[:, ch * D:(ch + 1) * D], src, 1.0,
                                               th[:, ch * D:(ch + 1) * D],
                                               op0=ALU.mult, op1=ALU.subtract)
                    vb.inc(i, 1)
        VB_WSPLIT = vb.n

        @block.sync
        def _(sync):
            i = sync.dma_start(out=s_cs1[:, :].rearrange("p (dc j n) -> p dc j n", j=6, n=128),
                               in_=cs1.rearrange("(dc p) j n -> p dc j n", p=128))
            d_in.dma(i)
        D_CS1 = d_in.n

        @block.tensor
        def _(t):
            t.wait_ge(vb.sem, VB_WSPLIT)
            t.wait_ge(d_in.sem, D_CS1)
            for oc in range(8):
                for dc in range(8):
                    base = dc * 6 * 128
                    cshi = s_cs1[:, base + 0 * 128: base + 1 * 128]
                    cslo = s_cs1[:, base + 1 * 128: base + 2 * 128]
                    sshi = s_cs1[:, base + 2 * 128: base + 3 * 128]
                    sslo = s_cs1[:, base + 3 * 128: base + 4 * 128]
                    snhi = s_cs1[:, base + 4 * 128: base + 5 * 128]
                    snlo = s_cs1[:, base + 5 * 128: base + 6 * 128]
                    wsl = slice(dc * D + oc * 128, dc * D + oc * 128 + 128)
                    wrh, wrl = s_wrh[:, wsl], s_wrl[:, wsl]
                    wih, wil = s_wih[:, wsl], s_wil[:, wsl]
                    outA = pB[:, oc * 128:(oc + 1) * 128]
                    outB = pB[:, 1024 + oc * 128: 1024 + (oc + 1) * 128]
                    prodsA = [(wrh, cshi), (wrh, cslo), (wrl, cshi),
                              (wih, sshi), (wih, sslo), (wil, sshi)]
                    prodsB = [(wih, cshi), (wih, cslo), (wil, cshi),
                              (wrh, snhi), (wrh, snlo), (wrl, snhi)]
                    for pi, (lh, rh) in enumerate(prodsA):
                        i = t.matmul(outA, lh, rh, start=(dc == 0 and pi == 0),
                                     stop=(dc == 7 and pi == 5))
                        if pi in (1, 4):
                            noldw(i)
                    for pi, (lh, rh) in enumerate(prodsB):
                        i = t.matmul(outB, lh, rh, start=(dc == 0 and pi == 0),
                                     stop=(dc == 7 and pi == 5))
                        if pi in (1, 4):
                            noldw(i)
            tb.inc(i, 1)
        TB_STAGE1 = tb.n

        @block.vector
        def _(v):
            v.wait_ge(tb.sem, TB_STAGE1)
            i = v.tensor_copy(s_abh[:, :], pB[:, 0:2048])
            vb.inc(i, 1)
            v.wait_ge(vb.sem, vb.n)
            i = v.scalar_tensor_tensor(s_abl[:, :], pB[:, 0:2048], 1.0, s_abh[:, :],
                                       op0=ALU.mult, op1=ALU.subtract)
            vb.inc(i, 1)
        VB_AB = vb.n

        @block.sync
        def _(sync):
            for oc in range(8):
                if oc >= 2:
                    sync.wait_ge(tb.sem, TB_STAGE1 + (oc - 1))
                for j in range(4):
                    i = sync.dma_start(
                        out=s_c2[:, ((oc % 2) * 4 + j) * D:((oc % 2) * 4 + j + 1) * D],
                        in_=cs2[j, oc * 128:(oc + 1) * 128, :])
                    d_in.dma(i)
        D_CS2_BASE = D_CS1

        @block.tensor
        def _(t):
            t.wait_ge(vb.sem, VB_AB)
            for oc in range(8):
                t.wait_ge(d_in.sem, D_CS2_BASE + 16 * 4 * (oc + 1))
                sl = (oc % 2) * 4
                c2hi = s_c2[:, (sl + 0) * D:(sl + 1) * D]
                c2lo = s_c2[:, (sl + 1) * D:(sl + 2) * D]
                sn2hi = s_c2[:, (sl + 2) * D:(sl + 3) * D]
                sn2lo = s_c2[:, (sl + 3) * D:(sl + 4) * D]
                Ah = s_abh[:, oc * 128:(oc + 1) * 128]
                Al = s_abl[:, oc * 128:(oc + 1) * 128]
                Bh = s_abh[:, 1024 + oc * 128:1024 + (oc + 1) * 128]
                Bl = s_abl[:, 1024 + oc * 128:1024 + (oc + 1) * 128]
                for nh in range(2):
                    om = pB[:, 2048 + nh * 512: 2048 + (nh + 1) * 512]
                    prods = [(Ah, c2hi), (Ah, c2lo), (Al, c2hi),
                             (Bh, sn2hi), (Bh, sn2lo), (Bl, sn2hi)]
                    for pi, (lh, rh) in enumerate(prods):
                        i = t.matmul(om, lh, rh[:, nh * 512:(nh + 1) * 512],
                                     start=(oc == 0 and pi == 0),
                                     stop=(oc == 7 and pi == 5))
                        if pi in (1, 4):
                            noldw(i)
                tb.inc(i, 1)
            t.wait_ge(d_in.sem, D_SETUP)
            t.wait_ge(d_g.sem, G_SETUP)
            t.transpose(pB[:, 3072:3072 + 21], s_bst[:, :], s_id[0:21, 0:21])
            t.matmul(pB[:, 3584:3584 + 256], s_onesr[:, :], s_brow[0:1, 0:256],
                     start=True, stop=True)
            i = t.matmul(pB[:, 3584 + 256:3584 + 512], s_onesr[:, :],
                         s_brow[0:1, 256:512], start=True, stop=True)
            tb.inc(i, 1)
        TB_STAGE2 = tb.n

        @block.scalar
        def _(s):
            s.wait_ge(tb.sem, TB_STAGE2)
            i = s.activation(s_msl[:, 0:D], pB[:, 2048:3072], AF.Identity, scale=0.5 / D)
            sb.inc(i, 1)
        SB_MHI = sb.n

        @block.vector
        def _(v):
            v.wait_ge(sb.sem, SB_MHI)
            v.scalar_tensor_tensor(s_msl[:, D:2 * D], pB[:, 2048:3072], 0.5 / D,
                                   s_msl[:, 0:D], op0=ALU.mult, op1=ALU.subtract)
            i = v.tensor_copy(s_bcols[:, 0:21], pB[:, 3072:3072 + 21])
            vb.inc(i, 1)
            v.wait_ge(vb.sem, vb.n)
            v.tensor_scalar_mul(s_bcols[:, 4:12], s_bcols[:, 4:12], 0.125)
            v.tensor_copy(s_bprojbc[:, 0:256], pB[:, 3584:3584 + 256])
            i = v.tensor_copy(s_bprojbc[:, 256:512], pB[:, 3584 + 256:3584 + 512])
            vb.inc(i, 1)
        VB_MLO = vb.n

        @block.tensor
        def _(t):
            t.wait_ge(vb.sem, VB_MLO)
            t.matmul(pB[:, 0:256], s_onesr[:, :], s_brow[0:1, 512:768],
                     start=True, stop=True)
            t.matmul(pB[:, 256:512], s_onesr[:, :], s_brow[0:1, 768:1024],
                     start=True, stop=True)
            t.matmul(pB[:, 512:1024], s_onesr[:, :], s_brow[0:1, D:D + 512],
                     start=True, stop=True)
            i = t.matmul(pB[:, 1024:1536], s_onesr[:, :],
                         s_brow[0:1, D + 512:2 * D], start=True, stop=True)
            tb.inc(i, 1)
        TB_BC2 = tb.n

        @block.vector
        def _(v):
            v.wait_ge(tb.sem, TB_BC2)
            v.tensor_copy(s_bprojbc[:, 512:768], pB[:, 0:256])
            v.tensor_copy(s_bprojbc[:, 768:1024], pB[:, 256:512])
            v.tensor_scalar_mul(s_fbhbc[:, 0:512], pB[:, 512:1024], 0.5)
            i = v.tensor_scalar_mul(s_fbhbc[:, 512:1024], pB[:, 1024:1536], 0.5)
            vb.inc(i, 1)
        VB_BIAS = vb.n

        @block.gpsimd
        def _(g):
            g.wait_ge(sb.sem, SB_MHI)
            g.wait_ge(vb.sem, VB_MLO)
            d_g.dma(g.dma_start(out=cc_m_in[0:128, :], in_=s_msl[:, 0:D]))
            d_g.dma(g.dma_start(out=cc_m_in[128:256, :], in_=s_msl[:, D:2 * D]))
            g.wait_ge(d_g.sem, d_g.n)
            i = g.collective_compute(
                "AllGather", ALU.bypass,
                replica_groups=[list(range(NCORES))],
                ins=[cc_m_in], outs=[cc_m_out])
            ccs.inc(i, 1)
        CC_M = ccs.n

        bctx.close()

        def emit_ln_stats(wait_sem, wait_vals):
            @block.vector
            def _(v):
                for tt in range(NXT):
                    if wait_vals is not None:
                        v.wait_ge(wait_sem, wait_vals[tt])
                    i = v.reduce_sum(s_stats[:, tt:tt + 1], sx[:, tt * D:(tt + 1) * D],
                                     axis=AX.X)
                    fV.inc(i, 1)

            @block.scalar
            def _(s):
                for tt in range(NXT):
                    if wait_vals is not None:
                        s.wait_ge(wait_sem, wait_vals[tt])
                    i = s.activation(s_sqdump[:, :], sx[:, tt * D:(tt + 1) * D], AF.Square,
                                     accum_out=s_stats[:, 16 + tt:17 + tt])
                    fS.inc(i, 1)
            s_stats_mark = fS.n

            sums_mark = fV.n

            @block.vector
            def _(v):
                v.wait_ge(fS.sem, s_stats_mark)
                v.wait_ge(fV.sem, sums_mark)
                v.tensor_scalar_mul(s_stats[:, 32:48], s_stats[:, 0:16], 1.0 / D)
                i = v.tensor_scalar_mul(s_stats[:, 16:32], s_stats[:, 16:32], 1.0 / D)
                fV.inc(i, 1)
                v.wait_ge(fV.sem, fV.n)
                i = v.tensor_mul(s_stats[:, 0:16], s_stats[:, 32:48], s_stats[:, 32:48])
                fV.inc(i, 1)
                v.wait_ge(fV.sem, fV.n)
                i = v.tensor_sub(s_stats[:, 16:32], s_stats[:, 16:32], s_stats[:, 0:16])
                fV.inc(i, 1)
                v.wait_ge(fV.sem, fV.n)
                i = v.tensor_scalar_add(s_stats[:, 16:32], s_stats[:, 16:32], EPS)
                fV.inc(i, 1)
            v_var = fV.n

            @block.scalar
            def _(s):
                s.wait_ge(fV.sem, v_var)
                i = s.activation(s_stats[:, 48:64], s_stats[:, 16:32], AF.Ln)
                fS.inc(i, 1)
                s.wait_ge(fS.sem, fS.n)
                i = s.activation(s_stats[:, 48:64], s_stats[:, 48:64], AF.Exp, scale=-0.5)
                fS.inc(i, 1)
            s_rstd = fS.n

            @block.vector
            def _(v):
                v.wait_ge(fS.sem, s_rstd)
                i = v.scalar_tensor_tensor(s_stats[:, 0:16], s_stats[:, 32:48], -1.0,
                                           s_stats[:, 48:64], op0=ALU.mult, op1=ALU.mult)
                fV.inc(i, 1)
            return fV.n

        def emit_ffn(tag, coef_mark, write_out, gate=None, load_gate=None):
            fctx = ExitStack()
            s_xlnTh = fctx.enter_context(nc.sbuf_tensor(f"s_xlnTh{tag}", [128, 8 * TL], BF16))
            s_xlnTl = fctx.enter_context(nc.sbuf_tensor(f"s_xlnTl{tag}", [128, 8 * TL], BF16))
            s_mh = fctx.enter_context(nc.sbuf_tensor(f"s_mh{tag}", [128, 8 * D], BF16))
            s_ml = fctx.enter_context(nc.sbuf_tensor(f"s_ml{tag}", [128, 8 * D], BF16))
            pF = fctx.enter_context(nc.psum_tensor(f"pF{tag}", [128, 3072], F32))

            base_fT = fT.n
            order = []
            for tt in range(NXT):
                order.append(("tr", tt))
                if tt >= 1:
                    order.append(("g", tt - 1, 0))
                    order.append(("g", tt - 1, 1))
            order.append(("g", NXT - 1, 0))
            order.append(("g", NXT - 1, 1))
            tr_mark, g_mark = {}, {}
            cnt = base_fT
            for it in order:
                cnt += 1
                if it[0] == "tr":
                    tr_mark[it[1]] = cnt
                else:
                    g_mark[(it[1], it[2])] = cnt

            base_fV = fV.n
            vorder = []
            for tt in range(NXT):
                vorder.append(("eT", tt))
                vorder.append(("ah", tt))
                if tt >= 1:
                    vorder.append(("eM", tt - 1, 0))
                    vorder.append(("eM", tt - 1, 1))
            vorder.append(("eM", NXT - 1, 0))
            vorder.append(("eM", NXT - 1, 1))
            eT_mark, ah_mark, eM_mark = {}, {}, {}
            vcnt = base_fV
            for it in vorder:
                if it[0] == "eT":
                    vcnt += 2
                    eT_mark[it[1]] = vcnt
                elif it[0] == "ah":
                    vcnt += 1
                    ah_mark[it[1]] = vcnt
                else:
                    vcnt += 1
                    eM_mark[(it[1], it[2])] = vcnt

            @block.gpsimd
            def _(g):
                g.wait_ge(ccs.sem, CC_M)
                if load_gate is not None:
                    g.wait_ge(load_gate[0], load_gate[1])
                for r in range(NCORES):
                    d_g.dma(g.dma_start(out=s_mh[:, r * D:(r + 1) * D],
                                        in_=cc_m_out[r * 256:r * 256 + 128, :]))
                    d_g.dma(g.dma_start(out=s_ml[:, r * D:(r + 1) * D],
                                        in_=cc_m_out[r * 256 + 128:r * 256 + 256, :]))
            G_MLOAD = d_g.n

            s_apply = []

            @block.scalar
            def _(s):
                for tt in range(NXT):
                    s.wait_ge(fV.sem, coef_mark)
                    if tt >= 2:
                        s.wait_ge(fT.sem, tr_mark[tt - 2])
                    i = s.activation(s_xlnf[:, (tt % 2) * D:(tt % 2 + 1) * D],
                                     sx[:, tt * D:(tt + 1) * D], AF.Identity,
                                     bias=s_stats[:, tt:tt + 1],
                                     scale=s_stats[:, 48 + tt:49 + tt])
                    s_apply.append(fS.inc(i, 1))

            @block.tensor
            def _(t):
                for it in order:
                    if it[0] == "tr":
                        tt = it[1]
                        t.wait_ge(fS.sem, s_apply[tt])
                        if tt == 0 and gate is not None:
                            t.wait_ge(gate[0], gate[1])
                        if tt >= 2:
                            t.wait_ge(fV.sem, eT_mark[tt - 2])
                        for c in range(8):
                            i = t.transpose(
                                pF[:, (tt % 2) * 1024 + c * 128:(tt % 2) * 1024 + (c + 1) * 128],
                                s_xlnf[:, (tt % 2) * D + c * 128:(tt % 2) * D + (c + 1) * 128],
                                s_id[:, :])
                        fT.inc(i, 1)
                    else:
                        tt, nh = it[1], it[2]
                        gidx = tt * 2 + nh
                        t.wait_ge(fV.sem, eT_mark[tt])
                        if gidx == 0:
                            t.wait_ge(d_g.sem, G_MLOAD)
                        if gidx >= 2:
                            t.wait_ge(fV.sem, eM_mark[((gidx - 2) // 2, (gidx - 2) % 2)])
                        om = pF[:, 2048 + (gidx % 2) * 512: 2048 + (gidx % 2 + 1) * 512]
                        for dc in range(8):
                            lhh = s_xlnTh[:, dc * TL + tt * 128: dc * TL + (tt + 1) * 128]
                            lhl = s_xlnTl[:, dc * TL + tt * 128: dc * TL + (tt + 1) * 128]
                            rhh = s_mh[:, dc * D + nh * 512: dc * D + (nh + 1) * 512]
                            rhl = s_ml[:, dc * D + nh * 512: dc * D + (nh + 1) * 512]
                            t.matmul(om, lhh, rhh, start=(dc == 0), stop=False)
                            noldw(t.matmul(om, lhh, rhl, start=False, stop=False))
                            i = t.matmul(om, lhl, rhh, start=False, stop=(dc == 7))
                        fT.inc(i, 1)

            @block.vector
            def _(v):
                for it in vorder:
                    if it[0] == "eT":
                        tt = it[1]
                        v.wait_ge(fT.sem, tr_mark[tt])
                        ps = pF[:, (tt % 2) * 1024:(tt % 2 + 1) * 1024]
                        dst_h = s_xlnTh[:, :].rearrange("p (dc t) -> p dc t", t=TL)[:, :, tt * 128:(tt + 1) * 128]
                        dst_l = s_xlnTl[:, :].rearrange("p (dc t) -> p dc t", t=TL)[:, :, tt * 128:(tt + 1) * 128]
                        src = ps.rearrange("p (dc t) -> p dc t", t=128)
                        i = v.tensor_copy(dst_h, src)
                        fV.inc(i, 1)
                        v.wait_ge(fV.sem, fV.n)
                        i = v.scalar_tensor_tensor(dst_l, src, 1.0, dst_h,
                                                   op0=ALU.mult, op1=ALU.subtract)
                        fV.inc(i, 1)
                    elif it[0] == "ah":
                        tt = it[1]
                        v.wait_ge(fS.sem, s_apply[tt])
                        i = v.tensor_add(sx[:, tt * D:(tt + 1) * D],
                                         sx[:, tt * D:(tt + 1) * D], s_fbhbc[:, :])
                        fV.inc(i, 1)
                    else:
                        tt, nh = it[1], it[2]
                        gidx = tt * 2 + nh
                        v.wait_ge(fT.sem, g_mark[(tt, nh)])
                        v.wait_ge(fV.sem, ah_mark[tt])
                        dst = sx[:, tt * D + nh * 512: tt * D + (nh + 1) * 512]
                        i = v.tensor_add(dst, dst,
                                         pF[:, 2048 + (gidx % 2) * 512: 2048 + (gidx % 2 + 1) * 512])
                        fV.inc(i, 1)

            if write_out:
                @block.sync
                def _(sync):
                    for tt in range(NXT):
                        sync.wait_ge(fV.sem, eM_mark[(tt, 1)])
                        d_out.dma(sync.dma_start(out=out_t[:, tt, :],
                                                 in_=sx[:, tt * D:(tt + 1) * D]))
            fctx.close()
            return eM_mark

        coef1 = emit_ln_stats(d_in.sem, [DX[tt // 4] for tt in range(NXT)])
        ffn1_done = emit_ffn("f1", coef1, write_out=(stage == 1),
                             gate=(vb.sem, VB_BIAS))
        if stage == 1:
            @block.gpsimd
            def _(g):
                g.wait_ge(d_out.sem, 16 * NXT)
            _apply_ldw_reuse(nc, RMM)
            return nc
        FFN1_ALL = ffn1_done[(NXT - 1, 1)]

        actx = ExitStack()
        s_qT = actx.enter_context(nc.sbuf_tensor("s_qT", [128, 8 * TL], BF16))
        s_ckvT = actx.enter_context(nc.sbuf_tensor("s_ckvT", [128, 8192], BF16))
        s_ckva = actx.enter_context(nc.sbuf_tensor("s_ckva", [128, NCH * 96], BF16))

        pctx = ExitStack()
        s_xlnTa = pctx.enter_context(nc.sbuf_tensor("s_xlnTa", [128, 8 * TL], BF16))
        s_wqdT = pctx.enter_context(nc.sbuf_tensor("s_wqdT", [128, 8 * QL], BF16))
        s_wqkT = pctx.enter_context(nc.sbuf_tensor("s_wqkT", [128, 4 * D], BF16))
        s_wkvdT = pctx.enter_context(nc.sbuf_tensor("s_wkvdT", [128, 8 * LAT], BF16))
        s_cqT = pctx.enter_context(nc.sbuf_tensor("s_cqT", [128, 4 * 512], BF16))
        s_ckvl = pctx.enter_context(nc.sbuf_tensor("s_ckvl", [LAT, TL], BF16))
        pP = pctx.enter_context(nc.psum_tensor("pP", [128, 2048], F32))
        pPb = pctx.enter_context(nc.psum_tensor("pPb", [128, 1024], BF16))

        coef2 = emit_ln_stats(fV.sem, [ffn1_done[(tt, 1)] for tt in range(NXT)])

        @block.gpsimd
        def _(g):
            g.wait_ge(fV.sem, FFN1_ALL)
            d_g.dma(g.dma_start(out=s_wqdT[:, :].rearrange("p (dc n) -> p dc n", n=QL),
                                in_=wqdT.rearrange("(dc p) n -> p dc n", p=128)))
            d_g.dma(g.dma_start(out=s_wqkT[:, :].rearrange("p (qc n) -> p qc n", n=D),
                                in_=wqkT.rearrange("(qc p) n -> p qc n", p=128)))
            d_g.dma(g.dma_start(out=s_wkvdT[:, :].rearrange("p (dc n) -> p dc n", n=LAT),
                                in_=wkvdT.rearrange("(dc p) n -> p dc n", p=128)))
        G_WATT = d_g.n

        s_apply2 = []

        @block.scalar
        def _(s):
            for tt in range(NXT):
                s.wait_ge(fV.sem, coef2)
                if tt >= 2:
                    s.wait_ge(pT.sem, tt - 1)
                i = s.activation(s_xlnf[:, (tt % 2) * D:(tt % 2 + 1) * D],
                                 sx[:, tt * D:(tt + 1) * D], AF.Identity,
                                 bias=s_stats[:, tt:tt + 1],
                                 scale=s_stats[:, 48 + tt:49 + tt])
                s_apply2.append(pS.inc(i, 1))
        PS_APPLY2 = pS.n

        @block.tensor
        def _(t):
            for tt in range(NXT):
                t.wait_ge(pS.sem, s_apply2[tt])
                if tt == 0:
                    t.wait_ge(fV.sem, FFN1_ALL)
                if tt >= 2:
                    t.wait_ge(pV.sem, tt - 1)
                for c in range(8):
                    i = t.transpose(
                        pP[:, (tt % 2) * 1024 + c * 128:(tt % 2) * 1024 + (c + 1) * 128],
                        s_xlnf[:, (tt % 2) * D + c * 128:(tt % 2) * D + (c + 1) * 128],
                        s_id[:, :])
                pT.inc(i, 1)
        PT_TRANSP = pT.n

        @block.vector
        def _(v):
            for tt in range(NXT):
                v.wait_ge(pT.sem, tt + 1)
                ps = pP[:, (tt % 2) * 1024:(tt % 2 + 1) * 1024]
                dst = s_xlnTa[:, :].rearrange("p (dc t) -> p dc t", t=TL)[:, :, tt * 128:(tt + 1) * 128]
                i = v.tensor_copy(dst, ps.rearrange("p (dc t) -> p dc t", t=128))
                pV.inc(i, 1)
        PV_EVA = pV.n

        def evac_mark(tq, pos):
            return PS_APPLY2 + tq * 13 + pos + 1

        @block.tensor
        def _(t):
            t.wait_ge(pV.sem, PV_EVA)
            t.wait_ge(d_g.sem, G_WATT)
            gi = 0
            for tq in range(4):
                if tq >= 1:
                    t.wait_ge(pS.sem, evac_mark(tq - 1, 0))
                for dc in range(8):
                    i = t.matmul(pP[0:64, 1024:1536],
                                 s_wkvdT[:, dc * LAT:(dc + 1) * LAT],
                                 s_xlnTa[:, dc * TL + tq * 512: dc * TL + (tq + 1) * 512],
                                 start=(dc == 0), stop=(dc == 7))
                pT.inc(i, 1)
                for qlc in range(4):
                    gi += 1
                    if gi > 2:
                        t.wait_ge(pS.sem, PS_APPLY2 + _gi_to_evac(gi - 2))
                    om = pP[:, ((gi % 2)) * 512:((gi % 2) + 1) * 512]
                    for dc in range(8):
                        i = t.matmul(om,
                                     s_wqdT[:, dc * QL + qlc * 128: dc * QL + qlc * 128 + 128],
                                     s_xlnTa[:, dc * TL + tq * 512: dc * TL + (tq + 1) * 512],
                                     start=(dc == 0), stop=(dc == 7))
                    pT.inc(i, 1)
                t.wait_ge(pS.sem, evac_mark(tq, 4))
                for hc in range(8):
                    gi += 1
                    if gi > 2:
                        t.wait_ge(pS.sem, PS_APPLY2 + _gi_to_evac(gi - 2))
                    om = pP[:, ((gi % 2)) * 512:((gi % 2) + 1) * 512]
                    for qlc in range(4):
                        i = t.matmul(om,
                                     s_wqkT[:, qlc * D + hc * 128: qlc * D + hc * 128 + 128],
                                     s_cqT[:, qlc * 512:(qlc + 1) * 512],
                                     start=(qlc == 0), stop=(qlc == 3))
                    pT.inc(i, 1)
        PT_QCHAIN = pT.n

        @block.scalar
        def _(s):
            tmark = PT_TRANSP
            for tq in range(4):
                tmark += 1
                s.wait_ge(pT.sem, tmark)
                i = s.activation(s_ckvl[:, tq * 512:(tq + 1) * 512],
                                 pP[0:64, 1024:1536], AF.Identity,
                                 bias=s_bcols[0:64, 12:13])
                pS.inc(i, 1)
                for qlc in range(4):
                    tmark += 1
                    s.wait_ge(pT.sem, tmark)
                    gi2 = tq * 12 + qlc + 1
                    om = pP[:, (gi2 % 2) * 512:(gi2 % 2 + 1) * 512]
                    i = s.activation(s_cqT[:, qlc * 512:(qlc + 1) * 512], om,
                                     AF.Identity, bias=s_bcols[:, qlc:qlc + 1])
                    pS.inc(i, 1)
                for hc in range(8):
                    tmark += 1
                    s.wait_ge(pT.sem, tmark)
                    gi2 = tq * 12 + 4 + hc + 1
                    om = pP[:, (gi2 % 2) * 512:(gi2 % 2 + 1) * 512]
                    i = s.activation(s_qT[:, hc * TL + tq * 512: hc * TL + (tq + 1) * 512],
                                     om, AF.Identity, scale=0.125,
                                     bias=s_bcols[:, 4 + hc:5 + hc])
                    pS.inc(i, 1)
        PS_PRE = pS.n

        @block.gpsimd
        def _(g):
            g.wait_ge(pS.sem, evac_mark(3, 0))
            d_g.dma(g.dma_start(out=cc_kv_in, in_=s_ckvl[:, :]))
            g.wait_ge(d_g.sem, d_g.n)
            i = g.collective_compute(
                "AllGather", ALU.bypass,
                replica_groups=[[0, 1, 2, 3], [4, 5, 6, 7]],
                ins=[cc_kv_in], outs=[cc_kv_out])
            ccs.inc(i, 1)
            g.wait_ge(ccs.sem, ccs.n)
            d_g.dma(g.dma_start(out=s_ckvT[0:64, :].rearrange("l (r t) -> l r t", t=TL),
                                in_=cc_kv_out.rearrange("(r l) t -> l r t", l=LAT)))
            d_g.dma(g.dma_start(out=s_ckvT[64:128, :].rearrange("l (r t) -> l r t", t=TL),
                                in_=cc_kv_out.rearrange("(r l) t -> l r t", l=LAT)))
        G_CKVT = d_g.n

        @block.tensor
        def _(t):
            t.wait_ge(d_g.sem, G_CKVT)
            for b4 in range(4):
                if b4 >= 1:
                    t.wait_ge(pV.sem, PV_EVA + b4)
                for i16 in range(16):
                    ch = b4 * 16 + i16
                    i = t.transpose(pPb[:, i16 * 64:(i16 + 1) * 64],
                                    s_ckvT[0:64, ch * 128:(ch + 1) * 128],
                                    s_idb[0:64, 0:64])
                pT.inc(i, 1)
        PT_KVT = pT.n

        @block.vector
        def _(v):
            ckva_v = s_ckva[:, :].rearrange("p (c w) -> p c w", w=96)
            for b4 in range(4):
                v.wait_ge(pT.sem, PT_KVT - 3 + b4)
                i = v.tensor_copy(ckva_v[:, b4 * 16:(b4 + 1) * 16, 0:64],
                                  pPb[:, :].rearrange("p (c w) -> p c w", w=64))
                pV.inc(i, 1)
            i = v.memset(ckva_v[:, :, 64:96], 1.0)
            pV.inc(i, 1)
        PV_KV = pV.n

        pctx.close()

        ATT = 128
        NAT = TL // ATT
        mctx = ExitStack()
        NPAR = 6
        s_attn = mctx.enter_context(nc.sbuf_tensor("s_attn", [128, NPAR * H * ATT], BF16))
        s_wvuT = mctx.enter_context(nc.sbuf_tensor("s_wvuT", [128, D], BF16))
        s_wprojT = mctx.enter_context(nc.sbuf_tensor("s_wprojT", [128, 8 * D], BF16))
        s_zn = mctx.enter_context(nc.sbuf_tensor("s_zn", [64, H * ATT], BF16))
        s_yT = mctx.enter_context(nc.sbuf_tensor("s_yT", [128, 8 * ATT], BF16))
        s_rsx = mctx.enter_context(nc.sbuf_tensor("s_rsx", [96, H * ATT], F32))
        s_bvubc = mctx.enter_context(nc.sbuf_tensor("s_bvubc", [128, 1024], F32))
        p_sc = mctx.enter_context(nc.psum_tensor("p_sc", [128, 2048], F32))
        p_zr = mctx.enter_context(nc.psum_tensor("p_zr", [128, 2048], F32))

        @block.gpsimd
        def _(g):
            g.wait_ge(pS.sem, PS_PRE)
            g.wait_ge(pT.sem, PT_QCHAIN)
            d_g.dma(g.dma_start(out=s_wvuT[:, :], in_=wvuT2))
            d_g.dma(g.dma_start(out=s_bvubc[:, :], in_=bvubc_d))
            d_g.dma(g.dma_start(out=s_wprojT[:, :].rearrange("p (cc n) -> p cc n", n=D),
                                in_=wprojT.rearrange("(cc p) n -> p cc n", p=128)))
        G_WMAIN = d_g.n

        mT_base = mT.n
        mV_base = mV.n

        def mTy(tt):
            return mT_base + tt * 3 + 1

        def mTpj(tt, nh):
            return mT_base + tt * 3 + 2 + nh

        def mVbp(tt):
            return mV_base + tt * 6 + 1

        def mVrec(tt):
            return mV_base + tt * 6 + 2

        def mVzn(tt):
            return mV_base + tt * 6 + 3

        def mVy(tt):
            return mV_base + tt * 6 + 4

        def mVpj(tt, nh):
            return mV_base + tt * 6 + 5 + nh

        G_MAIN0 = d_g.n

        def gRS(tt):
            return G_MAIN0 + 16 * 2 * (tt + 1)

        def emit_zrs(t, tt, cz):
            g = tt * NCH + cz
            t.wait_ge(aE.sem, tt * 2 * NCH + (cz + 1) * 2)
            if cz == 0 and tt > 0:
                pass
            base = (g % NPAR) * 2048
            lst = s_ckva[:, cz * 96:(cz + 1) * 96]
            first = cz == 0
            last = cz == NCH - 1
            for bj in range(4):
                if bj == 3:
                    aZ.inc(i, 1)
                    if cz == 0 and tt > 0:
                        t.wait_ge(mV.sem, mVpj(tt - 1, 1))
                i = t.matmul(p_zr[0:96, bj * 512:(bj + 1) * 512], lst,
                             s_attn[:, base + bj * 512:base + (bj + 1) * 512],
                             start=first, stop=last)
                if bj > 0:
                    noldw(i)
            aZ3.inc(i, 1)

        def emit_tail(t, tt):
            t.wait_ge(mV.sem, mVzn(tt))
            for m in range(NP):
                colA = (m // 4) * 1024 + (m % 4) * 128
                t.matmul(p_zr[0:64, m * ATT:(m + 1) * ATT],
                         s_wvuT[0:64, (2 * m) * 64:(2 * m + 1) * 64],
                         s_zn[0:64, colA:colA + ATT],
                         start=True, stop=True, tile_position=(0, 0))
                i = t.matmul(p_zr[64:128, m * ATT:(m + 1) * ATT],
                             s_wvuT[0:64, (2 * m + 1) * 64:(2 * m + 2) * 64],
                             s_zn[0:64, colA + 512:colA + 512 + ATT],
                             start=True, stop=True, tile_position=(0, 64))
            mT.inc(i, 1)
            t.wait_ge(mV.sem, mVy(tt))
            for nh in range(2):
                if nh == 1:
                    t.wait_ge(mV.sem, mVpj(tt, 0))
                for cc in range(8):
                    i = t.matmul(p_zr[:, 1536:2048],
                                 s_yT[:, cc * ATT:(cc + 1) * ATT],
                                 s_wprojT[:, cc * D + nh * 512: cc * D + (nh + 1) * 512],
                                 start=(cc == 0), stop=(cc == 7))
                mT.inc(i, 1)

        TAIL_AT = 5

        @block.tensor
        def _(t):
            t.wait_ge(d_g.sem, G_WMAIN)
            t.wait_ge(pV.sem, PV_KV)
            qA = s_qT[0:64, :].rearrange("p (m t2) -> p m t2", t2=TL)
            qB = s_qT[64:128, :].rearrange("p (m t2) -> p m t2", t2=TL)
            for tt in range(NAT):
                base_half = tt * 2 * NCH
                tsl = slice(tt * ATT, (tt + 1) * ATT)
                pend = [] if tt > 0 else None
                for c in range(NCH):
                    if c == 0:
                        if tt > 0:
                            t.wait_ge(aE.sem, base_half)
                    else:
                        t.wait_ge(aE.sem, base_half + 2 * c - 1)
                    lA = s_ckvT[0:64, c * 128:(c + 1) * 128]
                    lB = s_ckvT[64:128, c * 128:(c + 1) * 128]
                    t.matmul(p_sc[:, 0:512], lA, qA[:, 0:4, tsl],
                             start=True, stop=True, tile_position=(0, 0))
                    i = t.matmul(p_sc[:, 1024:1536], lB, qB[:, 0:4, tsl],
                                 start=True, stop=True, tile_position=(64, 0))
                    aS.inc(i, 1)
                    if c >= 1:
                        t.wait_ge(aE.sem, base_half + 2 * c)
                    t.matmul(p_sc[:, 512:1024], lA, qA[:, 4:8, tsl],
                             start=True, stop=True, tile_position=(0, 0))
                    i = t.matmul(p_sc[:, 1536:2048], lB, qB[:, 4:8, tsl],
                                 start=True, stop=True, tile_position=(64, 0))
                    aS.inc(i, 1)
                    if pend is not None and c == TAIL_AT:
                        emit_tail(t, tt - 1)
                        for cz in pend:
                            emit_zrs(t, tt, cz)
                        pend = None
                    if c >= 1:
                        if pend is not None:
                            pend.append(c - 1)
                        else:
                            emit_zrs(t, tt, c - 1)
                emit_zrs(t, tt, NCH - 1)
            emit_tail(t, NAT - 1)

        SB_BASE = sb.n

        def emit_recipS(s, tt):
            s.wait_ge(aZ.sem, (tt + 1) * NCH)
            s.wait_ge(aZ3.sem, (tt + 1) * NCH)
            i = s.activation(s_xlnf[64:96, 0:1024], p_zr[64:96, 1024:2048], AF.Ln)
            sb.inc(i, 1)
            s.wait_ge(sb.sem, sb.n)
            i = s.activation(s_rsx[64:96, 1024:2048], s_xlnf[64:96, 0:1024],
                             AF.Exp, scale=-1.0)
            sb.inc(i, 1)

        @block.scalar
        def _(s):
            for tt in range(NAT):
                base_half = tt * 2 * NCH
                for c in range(NCH):
                    g = tt * NCH + c
                    cp = g % NPAR
                    if tt > 0 and c == 1:
                        emit_recipS(s, tt - 1)
                    for hh in range(2):
                        gidx = c * 2 + hh
                        s.wait_ge(aS.sem, base_half + gidx + 1)
                        if hh == 0 and g >= NPAR:
                            s.wait_ge(aZ.sem, g - NPAR + 1)
                            s.wait_ge(aZ3.sem, g - NPAR + 1)
                        slot = hh
                        src = p_sc[:, :].rearrange("p (a x) -> p a x", a=2)[:, :, slot * 512:(slot + 1) * 512]
                        dst = s_attn[:, cp * 2048 + hh * 1024: cp * 2048 + (hh + 1) * 1024]
                        i = s.activation(dst.rearrange("p (a x) -> p a x", a=2), src, AF.Exp)
                        aE.inc(i, 1)
            emit_recipS(s, NAT - 1)

        @block.vector
        def _(v):
            for tt in range(NAT):
                v.wait_ge(pS.sem, s_apply2[tt])
                i = v.tensor_add(sx[:, tt * D:(tt + 1) * D],
                                 sx[:, tt * D:(tt + 1) * D], s_bprojbc[:, :])
                mV.inc(i, 1)
                v.wait_ge(aZ.sem, (tt + 1) * NCH)
                i = v.reciprocal(s_rsx[64:96, 0:1024], p_zr[64:96, 0:1024])
                mV.inc(i, 1)
                v.wait_ge(d_g.sem, gRS(tt))
                i = v.tensor_mul(s_zn[:, :], p_zr[0:64, :], s_rsx[0:64, :])
                mV.inc(i, 1)
                v.wait_ge(mT.sem, mTy(tt))
                i = v.tensor_add(s_yT[:, :], p_zr[:, 0:1024], s_bvubc[:, :])
                mV.inc(i, 1)
                for nh in range(2):
                    v.wait_ge(mT.sem, mTpj(tt, nh))
                    v.wait_ge(mV.sem, mVbp(tt))
                    dst = sx[:, tt * D + nh * 512: tt * D + (nh + 1) * 512]
                    i = v.tensor_add(dst, dst, p_zr[:, 1536:2048])
                    mV.inc(i, 1)

        @block.gpsimd
        def _(g):
            for tt in range(NAT):
                g.wait_ge(mV.sem, mVrec(tt))
                g.wait_ge(sb.sem, SB_BASE + 2 * (tt + 1))
                d_g.dma(g.dma_start(
                    out=rs_dram.rearrange("(o hb) t -> o hb t", o=1),
                    in_=s_rsx[64:65, :].rearrange("p (hb t) -> p hb t", t=ATT)))
                g.wait_ge(d_g.sem, d_g.n)
                d_g.dma(g.dma_start(
                    out=s_rsx[0:64, :].rearrange("p (hb t) -> p hb t", t=ATT),
                    in_=rs_dram.rearrange("(o hb) t -> o hb t", o=1)[0:1].partition_broadcast(64)))

        mctx.close()
        actx.close()

        if stage == 2:
            @block.sync
            def _(sync):
                for tt in range(NXT):
                    sync.wait_ge(mV.sem, mVpj(tt, 1))
                    d_out.dma(sync.dma_start(out=out_t[:, tt, :],
                                             in_=sx[:, tt * D:(tt + 1) * D]))

            @block.gpsimd
            def _(g):
                g.wait_ge(d_out.sem, 16 * NXT)
            _apply_ldw_reuse(nc, RMM)
            return nc

        ATT_ALL = mVpj(NXT - 1, 1)
        coef3 = emit_ln_stats(mV.sem, [mVpj(tt, 1) for tt in range(NXT)])
        emit_ffn("f2", coef3, write_out=True,
                 gate=(mV.sem, ATT_ALL), load_gate=(mV.sem, ATT_ALL))

        @block.gpsimd
        def _(g):
            g.wait_ge(d_out.sem, 16 * NXT)

    _apply_ldw_reuse(nc, RMM)
    return nc


def _gi_to_evac(gi):
    tq = (gi - 1) // 12
    pos = (gi - 1) % 12
    return tq * 13 + 2 + pos


def _prep_inputs(inputs):
    cs1, cs2, ident, identb = _host_consts()
    x = np.asarray(inputs["L0_attn_x"], np.float32)
    x_flat = np.ascontiguousarray(x.reshape(16 * 1024, D))
    wqdT = np.ascontiguousarray(np.asarray(inputs["L0_attn_wq_d"], np.float32).T)
    wqkT = np.ascontiguousarray(np.asarray(inputs["L0_attn_w_qk"], np.float32).T)
    wkvdT = np.ascontiguousarray(np.asarray(inputs["L0_attn_wkv_d"], np.float32).T)
    wvuT = np.ascontiguousarray(np.asarray(inputs["L0_attn_wv_u"], np.float32).T)
    wvuT2 = np.ascontiguousarray(np.concatenate([wvuT, wvuT], axis=0))
    wprojT = np.ascontiguousarray(np.asarray(inputs["L0_attn_w_proj"], np.float32).T)
    bq = np.asarray(inputs["L0_attn_bq_d"], np.float32).reshape(4, 128)
    bqk = np.asarray(inputs["L0_attn_b_qk"], np.float32).reshape(8, 128)
    bkv = np.zeros((1, 128), np.float32)
    bkv[0, :64] = np.asarray(inputs["L0_attn_bkv_d"], np.float32)
    bvu = np.asarray(inputs["L0_attn_bv_u"], np.float32).reshape(8, 128)
    bstack = np.ascontiguousarray(np.concatenate([bq, bqk, bkv, bvu], axis=0))
    bvubc = np.ascontiguousarray(np.concatenate(
        [np.repeat(bvu[m][:, None], 128, axis=1) for m in range(8)], axis=1))
    bprow = np.ascontiguousarray(np.asarray(inputs["L0_attn_b_proj"], np.float32).reshape(1, D))
    fbrow = np.ascontiguousarray(np.asarray(inputs["L0_ffn_bias"], np.float32).reshape(1, D))
    wfr = np.ascontiguousarray(np.asarray(inputs["L0_ffn_w_freq_real"], np.float32))
    wfi = np.ascontiguousarray(np.asarray(inputs["L0_ffn_w_freq_imag"], np.float32))

    in_maps = []
    for c in range(NCORES):
        shard = np.ascontiguousarray(x_flat[c * TL:(c + 1) * TL])
        in_maps.append({
            "xs": shard, "wqdT": wqdT, "wqkT": wqkT, "wkvdT": wkvdT,
            "wvuT2": wvuT2, "wprojT": wprojT, "wfr": wfr, "wfi": wfi,
            "bvubc": bvubc,
            "cs1": cs1[c], "cs2": cs2, "bstack": bstack, "bprow": bprow,
            "fbrow": fbrow, "identd": ident, "identbd": identb,
        })
    return in_maps


def run(inputs, stage=3):
    key = ("nc", stage)
    if key not in _NC_CACHE:
        _NC_CACHE[key] = build_nc(stage)
    nc = _NC_CACHE[key]
    in_maps = _prep_inputs(inputs)
    res = run_bass_kernel_spmd(nc, in_maps, core_ids=list(range(NCORES)))
    outs = np.concatenate([np.asarray(res.results[c]["out"], np.float32)
                           for c in range(NCORES)], axis=0)
    return outs.reshape(2, 8192, D)


def kernel(**inputs):
    return run(inputs, stage=3)


# version 11
# speedup vs baseline: 1.0451x; 1.0451x over previous
import numpy as np
import ml_dtypes
from contextlib import ExitStack

import concourse.bass as bass
import concourse.mybir as mybir
from concourse.bass_utils import run_bass_kernel_spmd

F32 = mybir.dt.float32
BF16 = mybir.dt.bfloat16
AF = mybir.ActivationFunctionType
ALU = mybir.AluOpType
AX = mybir.AxisListType

D = 1024
TL = 2048
NXT = 16
H = 16
LAT = 64
QL = 512
TT = 256
NTT = TL // TT
NCH = 64
NP = H // 2
PPT = NCH * NP
EPS = 1e-5
NCORES = 8

_NC_CACHE = {}


def _apply_ldw_reuse(nc, names):
    names = set(names)
    for f in nc.m.functions:
        for bb in f.blocks:
            il = bb.instructions
            changed = False
            for k, ins in enumerate(il):
                if type(ins).__name__ == "InstMatmult" and ins.name in names:
                    clone = mybir.InstMatmult(
                        name=ins.name, engine=ins.engine,
                        ins=list(ins.ins), outs=list(ins.outs),
                        replication_resolution=ins.replication_resolution,
                        replication_shift_amnt=ins.replication_shift_amnt,
                        replication_num_rows=ins.replication_num_rows,
                        start_tensor_calc=ins.start_tensor_calc,
                        stop_tensor_calc=ins.stop_tensor_calc,
                        is_transpose=ins.is_transpose,
                        tile_position=ins.tile_position,
                        tile_size=ins.tile_size,
                        perf_mode=ins.perf_mode,
                        bass_skip_group_check=ins.bass_skip_group_check,
                        ldweights=False,
                    )
                    si = ins.sync_info
                    if si is not None:
                        clone.sync_info = si
                    il[k] = clone
                    changed = True
            if changed:
                bb.instructions = il


def _bf(x):
    return np.asarray(x).astype(ml_dtypes.bfloat16)


def _host_consts():
    if "consts" in _NC_CACHE:
        return _NC_CACHE["consts"]
    n = np.arange(D, dtype=np.float64)
    ang = 2.0 * np.pi * np.outer(n, n) / D
    C = np.cos(ang)
    S = np.sin(ang)

    def split(v):
        hi = _bf(v)
        lo = _bf(v - hi.astype(np.float64))
        return hi, lo

    C_hi, C_lo = split(C)
    S_hi, S_lo = split(S)
    Sn_hi, Sn_lo = -S_hi, -S_lo
    cs1 = []
    for r in range(NCORES):
        sl = slice(r * 128, (r + 1) * 128)
        cs1.append(np.ascontiguousarray(np.stack(
            [C_hi[:, sl], C_lo[:, sl], S_hi[:, sl], S_lo[:, sl],
             Sn_hi[:, sl], Sn_lo[:, sl]], axis=1)))
    cs2 = np.ascontiguousarray(np.stack([C_hi, C_lo, Sn_hi, Sn_lo], axis=0))
    ident = np.eye(128, dtype=np.float32)
    identb = _bf(np.eye(128, dtype=np.float32))
    out = (cs1, cs2, ident, identb)
    _NC_CACHE["consts"] = out
    return out


class Ctr:
    def __init__(self, nc, ctx, name):
        self.sem = ctx.enter_context(nc.semaphore(name))
        self.n = 0

    def inc(self, inst, k=1):
        inst.then_inc(self.sem, k)
        self.n += k
        return self.n

    def dma(self, inst):
        return self.inc(inst, 16)


def build_nc(stage=3):
    nc = bass.Bass("TRN2", target_bir_lowering=False)
    RMM = []

    def noldw(i):
        RMM.append(i.ins.name)
        return i
    dt = nc.dram_tensor
    xs = dt("xs", [TL, D], F32, kind="ExternalInput").ap()
    wqdT = dt("wqdT", [D, QL], F32, kind="ExternalInput").ap()
    wqkT = dt("wqkT", [QL, D], F32, kind="ExternalInput").ap()
    wkvdT = dt("wkvdT", [D, LAT], F32, kind="ExternalInput").ap()
    wvuT2 = dt("wvuT2", [128, D], F32, kind="ExternalInput").ap()
    wprojT = dt("wprojT", [D, D], F32, kind="ExternalInput").ap()
    wfr = dt("wfr", [D, D], F32, kind="ExternalInput").ap()
    wfi = dt("wfi", [D, D], F32, kind="ExternalInput").ap()
    cs1 = dt("cs1", [D, 6, 128], BF16, kind="ExternalInput").ap()
    cs2 = dt("cs2", [4, D, D], BF16, kind="ExternalInput").ap()
    bstack = dt("bstack", [21, 128], F32, kind="ExternalInput").ap()
    bprow = dt("bprow", [1, D], F32, kind="ExternalInput").ap()
    fbrow = dt("fbrow", [1, D], F32, kind="ExternalInput").ap()
    bvubc_d = dt("bvubc", [128, 1024], F32, kind="ExternalInput").ap()
    identd = dt("identd", [128, 128], F32, kind="ExternalInput").ap()
    identbd = dt("identbd", [128, 128], BF16, kind="ExternalInput").ap()
    out = dt("out", [TL, D], F32, kind="ExternalOutput").ap()

    cc_m_in = dt("cc_m_in", [256, D], BF16).ap()
    cc_m_out = dt("cc_m_out", [2048, D], BF16, addr_space="Shared").ap()
    cc_kv_in = dt("cc_kv_in", [LAT, TL], BF16).ap()
    cc_kv_out = dt("cc_kv_out", [4 * LAT, TL], BF16).ap()
    rs_dram = dt("rs_dram", [16, 128], F32).ap()

    xs_t = xs.rearrange("(tt p) d -> p tt d", p=128)
    out_t = out.rearrange("(tt p) d -> p tt d", p=128)

    with ExitStack() as CTX:
        d_in = Ctr(nc, CTX, "d_in")
        d_g = Ctr(nc, CTX, "d_g")
        ccs = Ctr(nc, CTX, "ccs")
        d_out = Ctr(nc, CTX, "d_out")
        tb = Ctr(nc, CTX, "tb")
        vb = Ctr(nc, CTX, "vb")
        sb = Ctr(nc, CTX, "sb")
        fS = Ctr(nc, CTX, "fS")
        fT = Ctr(nc, CTX, "fT")
        fV = Ctr(nc, CTX, "fV")
        pT = Ctr(nc, CTX, "pT")
        pS = Ctr(nc, CTX, "pS")
        pV = Ctr(nc, CTX, "pV")
        aS = Ctr(nc, CTX, "aS")
        aE = Ctr(nc, CTX, "aE")
        aZ = Ctr(nc, CTX, "aZ")
        aZ3 = Ctr(nc, CTX, "aZ3")
        mT = Ctr(nc, CTX, "mT")
        mV = Ctr(nc, CTX, "mV")

        sx = CTX.enter_context(nc.sbuf_tensor("sx", [128, NXT * D], F32))
        s_stats = CTX.enter_context(nc.sbuf_tensor("s_stats", [128, 64], F32))
        s_bcols = CTX.enter_context(nc.sbuf_tensor("s_bcols", [128, 24], F32))
        s_bprojbc = CTX.enter_context(nc.sbuf_tensor("s_bprojbc", [128, D], F32))
        s_fbhbc = CTX.enter_context(nc.sbuf_tensor("s_fbhbc", [128, D], F32))
        s_ones32 = CTX.enter_context(nc.sbuf_tensor("s_ones32", [128, 32], BF16))
        s_onesr = CTX.enter_context(nc.sbuf_tensor("s_onesr", [1, 128], F32))
        s_id = CTX.enter_context(nc.sbuf_tensor("s_id", [128, 128], F32))
        s_idb = CTX.enter_context(nc.sbuf_tensor("s_idb", [128, 128], BF16))
        s_xlnf = CTX.enter_context(nc.sbuf_tensor("s_xlnf", [128, 2 * D], F32))
        s_sqdump = CTX.enter_context(nc.sbuf_tensor("s_sqdump", [128, D], BF16))

        bctx = ExitStack()
        s_brow = bctx.enter_context(nc.sbuf_tensor("s_brow", [1, 2 * D], F32))
        s_bst = bctx.enter_context(nc.sbuf_tensor("s_bst", [21, 128], F32))

        block = CTX.enter_context(nc.Block())

        @block.sync
        def _(sync):
            sxv = sx[:, :].rearrange("p (tt d) -> p tt d", d=D)
            for q in range(4):
                i = sync.dma_start(out=sxv[:, q * 4:(q + 1) * 4, :],
                                   in_=xs_t[:, q * 4:(q + 1) * 4, :])
                d_in.dma(i)
            d_in.dma(sync.dma_start(out=s_id[:, :], in_=identd))
            d_in.dma(sync.dma_start(out=s_idb[:, :], in_=identbd))
            d_in.dma(sync.dma_start(out=s_bst[:, :], in_=bstack))
            d_in.dma(sync.dma_start(out=s_brow[0:1, 0:D], in_=bprow))
            d_in.dma(sync.dma_start(out=s_brow[0:1, D:2 * D], in_=fbrow))
        DX = [16 * (q + 1) for q in range(4)]
        D_SETUP = d_in.n

        @block.gpsimd
        def _(g):
            g.memset(s_ones32[:, :], 1.0)
            i = g.memset(s_onesr[:, :], 1.0)
            d_g.inc(i, 1)
        G_SETUP = d_g.n

        s_w32 = bctx.enter_context(nc.sbuf_tensor("s_w32", [128, 2 * D], F32))
        s_wrh = bctx.enter_context(nc.sbuf_tensor("s_wrh", [128, 8 * D], BF16))
        s_wrl = bctx.enter_context(nc.sbuf_tensor("s_wrl", [128, 8 * D], BF16))
        s_wih = bctx.enter_context(nc.sbuf_tensor("s_wih", [128, 8 * D], BF16))
        s_wil = bctx.enter_context(nc.sbuf_tensor("s_wil", [128, 8 * D], BF16))
        s_cs1 = bctx.enter_context(nc.sbuf_tensor("s_cs1", [128, 8 * 6 * 128], BF16))
        s_abh = bctx.enter_context(nc.sbuf_tensor("s_abh", [128, 2048], BF16))
        s_abl = bctx.enter_context(nc.sbuf_tensor("s_abl", [128, 2048], BF16))
        s_c2 = bctx.enter_context(nc.sbuf_tensor("s_c2", [128, 2 * 4 * D], BF16))
        s_msl = bctx.enter_context(nc.sbuf_tensor("s_msl", [128, 2 * D], BF16))
        pB = bctx.enter_context(nc.psum_tensor("pB", [128, 4096], F32))

        @block.gpsimd
        def _(g):
            for wi, wsrc in enumerate((wfr, wfi)):
                for ch in range(8):
                    k = wi * 8 + ch
                    if k >= 2:
                        g.wait_ge(vb.sem, 2 * (k - 1))
                    i = g.dma_start(out=s_w32[:, (k % 2) * D:(k % 2 + 1) * D],
                                    in_=wsrc[ch * 128:(ch + 1) * 128, :])
                    d_g.dma(i)

        @block.vector
        def _(v):
            for wi in range(2):
                th, tl = (s_wrh, s_wrl) if wi == 0 else (s_wih, s_wil)
                for ch in range(8):
                    k = wi * 8 + ch
                    v.wait_ge(d_g.sem, G_SETUP + 16 * (k + 1))
                    src = s_w32[:, (k % 2) * D:(k % 2 + 1) * D]
                    i = v.tensor_copy(th[:, ch * D:(ch + 1) * D], src)
                    vb.inc(i, 1)
                    v.wait_ge(vb.sem, vb.n)
                    i = v.scalar_tensor_tensor(tl[:, ch * D:(ch + 1) * D], src, 1.0,
                                               th[:, ch * D:(ch + 1) * D],
                                               op0=ALU.mult, op1=ALU.subtract)
                    vb.inc(i, 1)
        VB_WSPLIT = vb.n

        @block.sync
        def _(sync):
            i = sync.dma_start(out=s_cs1[:, :].rearrange("p (dc j n) -> p dc j n", j=6, n=128),
                               in_=cs1.rearrange("(dc p) j n -> p dc j n", p=128))
            d_in.dma(i)
        D_CS1 = d_in.n

        @block.tensor
        def _(t):
            t.wait_ge(vb.sem, VB_WSPLIT)
            t.wait_ge(d_in.sem, D_CS1)
            for oc in range(8):
                for dc in range(8):
                    base = dc * 6 * 128
                    cshi = s_cs1[:, base + 0 * 128: base + 1 * 128]
                    cslo = s_cs1[:, base + 1 * 128: base + 2 * 128]
                    sshi = s_cs1[:, base + 2 * 128: base + 3 * 128]
                    sslo = s_cs1[:, base + 3 * 128: base + 4 * 128]
                    snhi = s_cs1[:, base + 4 * 128: base + 5 * 128]
                    snlo = s_cs1[:, base + 5 * 128: base + 6 * 128]
                    wsl = slice(dc * D + oc * 128, dc * D + oc * 128 + 128)
                    wrh, wrl = s_wrh[:, wsl], s_wrl[:, wsl]
                    wih, wil = s_wih[:, wsl], s_wil[:, wsl]
                    outA = pB[:, oc * 128:(oc + 1) * 128]
                    outB = pB[:, 1024 + oc * 128: 1024 + (oc + 1) * 128]
                    prodsA = [(wrh, cshi), (wrh, cslo), (wrl, cshi),
                              (wih, sshi), (wih, sslo), (wil, sshi)]
                    prodsB = [(wih, cshi), (wih, cslo), (wil, cshi),
                              (wrh, snhi), (wrh, snlo), (wrl, snhi)]
                    for pi, (lh, rh) in enumerate(prodsA):
                        i = t.matmul(outA, lh, rh, start=(dc == 0 and pi == 0),
                                     stop=(dc == 7 and pi == 5))
                        if pi in (1, 4):
                            noldw(i)
                    for pi, (lh, rh) in enumerate(prodsB):
                        i = t.matmul(outB, lh, rh, start=(dc == 0 and pi == 0),
                                     stop=(dc == 7 and pi == 5))
                        if pi in (1, 4):
                            noldw(i)
            tb.inc(i, 1)
        TB_STAGE1 = tb.n

        @block.vector
        def _(v):
            v.wait_ge(tb.sem, TB_STAGE1)
            i = v.tensor_copy(s_abh[:, :], pB[:, 0:2048])
            vb.inc(i, 1)
            v.wait_ge(vb.sem, vb.n)
            i = v.scalar_tensor_tensor(s_abl[:, :], pB[:, 0:2048], 1.0, s_abh[:, :],
                                       op0=ALU.mult, op1=ALU.subtract)
            vb.inc(i, 1)
        VB_AB = vb.n

        @block.sync
        def _(sync):
            for oc in range(8):
                if oc >= 2:
                    sync.wait_ge(tb.sem, TB_STAGE1 + (oc - 1))
                for j in range(4):
                    i = sync.dma_start(
                        out=s_c2[:, ((oc % 2) * 4 + j) * D:((oc % 2) * 4 + j + 1) * D],
                        in_=cs2[j, oc * 128:(oc + 1) * 128, :])
                    d_in.dma(i)
        D_CS2_BASE = D_CS1

        @block.tensor
        def _(t):
            t.wait_ge(vb.sem, VB_AB)
            for oc in range(8):
                t.wait_ge(d_in.sem, D_CS2_BASE + 16 * 4 * (oc + 1))
                sl = (oc % 2) * 4
                c2hi = s_c2[:, (sl + 0) * D:(sl + 1) * D]
                c2lo = s_c2[:, (sl + 1) * D:(sl + 2) * D]
                sn2hi = s_c2[:, (sl + 2) * D:(sl + 3) * D]
                sn2lo = s_c2[:, (sl + 3) * D:(sl + 4) * D]
                Ah = s_abh[:, oc * 128:(oc + 1) * 128]
                Al = s_abl[:, oc * 128:(oc + 1) * 128]
                Bh = s_abh[:, 1024 + oc * 128:1024 + (oc + 1) * 128]
                Bl = s_abl[:, 1024 + oc * 128:1024 + (oc + 1) * 128]
                for nh in range(2):
                    om = pB[:, 2048 + nh * 512: 2048 + (nh + 1) * 512]
                    prods = [(Ah, c2hi), (Ah, c2lo), (Al, c2hi),
                             (Bh, sn2hi), (Bh, sn2lo), (Bl, sn2hi)]
                    for pi, (lh, rh) in enumerate(prods):
                        i = t.matmul(om, lh, rh[:, nh * 512:(nh + 1) * 512],
                                     start=(oc == 0 and pi == 0),
                                     stop=(oc == 7 and pi == 5))
                        if pi in (1, 4):
                            noldw(i)
                tb.inc(i, 1)
            t.wait_ge(d_in.sem, D_SETUP)
            t.wait_ge(d_g.sem, G_SETUP)
            t.transpose(pB[:, 3072:3072 + 21], s_bst[:, :], s_id[0:21, 0:21])
            t.matmul(pB[:, 3584:3584 + 256], s_onesr[:, :], s_brow[0:1, 0:256],
                     start=True, stop=True)
            i = t.matmul(pB[:, 3584 + 256:3584 + 512], s_onesr[:, :],
                         s_brow[0:1, 256:512], start=True, stop=True)
            tb.inc(i, 1)
        TB_STAGE2 = tb.n

        @block.scalar
        def _(s):
            s.wait_ge(tb.sem, TB_STAGE2)
            i = s.activation(s_msl[:, 0:D], pB[:, 2048:3072], AF.Identity, scale=0.5 / D)
            sb.inc(i, 1)
        SB_MHI = sb.n

        @block.vector
        def _(v):
            v.wait_ge(sb.sem, SB_MHI)
            v.scalar_tensor_tensor(s_msl[:, D:2 * D], pB[:, 2048:3072], 0.5 / D,
                                   s_msl[:, 0:D], op0=ALU.mult, op1=ALU.subtract)
            i = v.tensor_copy(s_bcols[:, 0:21], pB[:, 3072:3072 + 21])
            vb.inc(i, 1)
            v.wait_ge(vb.sem, vb.n)
            v.tensor_scalar_mul(s_bcols[:, 4:12], s_bcols[:, 4:12], 0.125)
            v.tensor_copy(s_bprojbc[:, 0:256], pB[:, 3584:3584 + 256])
            i = v.tensor_copy(s_bprojbc[:, 256:512], pB[:, 3584 + 256:3584 + 512])
            vb.inc(i, 1)
        VB_MLO = vb.n

        @block.tensor
        def _(t):
            t.wait_ge(vb.sem, VB_MLO)
            t.matmul(pB[:, 0:256], s_onesr[:, :], s_brow[0:1, 512:768],
                     start=True, stop=True)
            t.matmul(pB[:, 256:512], s_onesr[:, :], s_brow[0:1, 768:1024],
                     start=True, stop=True)
            t.matmul(pB[:, 512:1024], s_onesr[:, :], s_brow[0:1, D:D + 512],
                     start=True, stop=True)
            i = t.matmul(pB[:, 1024:1536], s_onesr[:, :],
                         s_brow[0:1, D + 512:2 * D], start=True, stop=True)
            tb.inc(i, 1)
        TB_BC2 = tb.n

        @block.vector
        def _(v):
            v.wait_ge(tb.sem, TB_BC2)
            v.tensor_copy(s_bprojbc[:, 512:768], pB[:, 0:256])
            v.tensor_copy(s_bprojbc[:, 768:1024], pB[:, 256:512])
            v.tensor_scalar_mul(s_fbhbc[:, 0:512], pB[:, 512:1024], 0.5)
            i = v.tensor_scalar_mul(s_fbhbc[:, 512:1024], pB[:, 1024:1536], 0.5)
            vb.inc(i, 1)
        VB_BIAS = vb.n

        @block.gpsimd
        def _(g):
            g.wait_ge(sb.sem, SB_MHI)
            g.wait_ge(vb.sem, VB_MLO)
            d_g.dma(g.dma_start(out=cc_m_in[0:128, :], in_=s_msl[:, 0:D]))
            d_g.dma(g.dma_start(out=cc_m_in[128:256, :], in_=s_msl[:, D:2 * D]))
            g.wait_ge(d_g.sem, d_g.n)
            i = g.collective_compute(
                "AllGather", ALU.bypass,
                replica_groups=[list(range(NCORES))],
                ins=[cc_m_in], outs=[cc_m_out])
            ccs.inc(i, 1)
        CC_M = ccs.n

        bctx.close()

        def emit_ln_stats(wait_sem, wait_vals):
            @block.vector
            def _(v):
                for tt in range(NXT):
                    if wait_vals is not None:
                        v.wait_ge(wait_sem, wait_vals[tt])
                    i = v.reduce_sum(s_stats[:, tt:tt + 1], sx[:, tt * D:(tt + 1) * D],
                                     axis=AX.X)
                    fV.inc(i, 1)

            @block.scalar
            def _(s):
                for tt in range(NXT):
                    if wait_vals is not None:
                        s.wait_ge(wait_sem, wait_vals[tt])
                    i = s.activation(s_sqdump[:, :], sx[:, tt * D:(tt + 1) * D], AF.Square,
                                     accum_out=s_stats[:, 16 + tt:17 + tt])
                    fS.inc(i, 1)
            s_stats_mark = fS.n

            sums_mark = fV.n

            @block.vector
            def _(v):
                v.wait_ge(fS.sem, s_stats_mark)
                v.wait_ge(fV.sem, sums_mark)
                v.tensor_scalar_mul(s_stats[:, 32:48], s_stats[:, 0:16], 1.0 / D)
                i = v.tensor_scalar_mul(s_stats[:, 16:32], s_stats[:, 16:32], 1.0 / D)
                fV.inc(i, 1)
                v.wait_ge(fV.sem, fV.n)
                i = v.tensor_mul(s_stats[:, 0:16], s_stats[:, 32:48], s_stats[:, 32:48])
                fV.inc(i, 1)
                v.wait_ge(fV.sem, fV.n)
                i = v.tensor_sub(s_stats[:, 16:32], s_stats[:, 16:32], s_stats[:, 0:16])
                fV.inc(i, 1)
                v.wait_ge(fV.sem, fV.n)
                i = v.tensor_scalar_add(s_stats[:, 16:32], s_stats[:, 16:32], EPS)
                fV.inc(i, 1)
            v_var = fV.n

            @block.scalar
            def _(s):
                s.wait_ge(fV.sem, v_var)
                i = s.activation(s_stats[:, 48:64], s_stats[:, 16:32], AF.Ln)
                fS.inc(i, 1)
                s.wait_ge(fS.sem, fS.n)
                i = s.activation(s_stats[:, 48:64], s_stats[:, 48:64], AF.Exp, scale=-0.5)
                fS.inc(i, 1)
            s_rstd = fS.n

            @block.vector
            def _(v):
                v.wait_ge(fS.sem, s_rstd)
                i = v.scalar_tensor_tensor(s_stats[:, 0:16], s_stats[:, 32:48], -1.0,
                                           s_stats[:, 48:64], op0=ALU.mult, op1=ALU.mult)
                fV.inc(i, 1)
            return fV.n

        def emit_ffn(tag, coef_mark, write_out, gate=None, load_gate=None):
            fctx = ExitStack()
            s_xlnTh = fctx.enter_context(nc.sbuf_tensor(f"s_xlnTh{tag}", [128, 8 * TL], BF16))
            s_xlnTl = fctx.enter_context(nc.sbuf_tensor(f"s_xlnTl{tag}", [128, 8 * TL], BF16))
            s_mh = fctx.enter_context(nc.sbuf_tensor(f"s_mh{tag}", [128, 8 * D], BF16))
            s_ml = fctx.enter_context(nc.sbuf_tensor(f"s_ml{tag}", [128, 8 * D], BF16))
            pF = fctx.enter_context(nc.psum_tensor(f"pF{tag}", [128, 3072], F32))

            base_fT = fT.n
            order = []
            for tt in range(NXT):
                order.append(("tr", tt))
                if tt >= 1:
                    order.append(("g", tt - 1, 0))
                    order.append(("g", tt - 1, 1))
            order.append(("g", NXT - 1, 0))
            order.append(("g", NXT - 1, 1))
            tr_mark, g_mark = {}, {}
            cnt = base_fT
            for it in order:
                cnt += 1
                if it[0] == "tr":
                    tr_mark[it[1]] = cnt
                else:
                    g_mark[(it[1], it[2])] = cnt

            base_fV = fV.n
            vorder = []
            for tt in range(NXT):
                vorder.append(("eT", tt))
                vorder.append(("ah", tt))
                if tt >= 1:
                    vorder.append(("eM", tt - 1, 0))
                    vorder.append(("eM", tt - 1, 1))
            vorder.append(("eM", NXT - 1, 0))
            vorder.append(("eM", NXT - 1, 1))
            eT_mark, ah_mark, eM_mark = {}, {}, {}
            vcnt = base_fV
            for it in vorder:
                if it[0] == "eT":
                    vcnt += 2
                    eT_mark[it[1]] = vcnt
                elif it[0] == "ah":
                    vcnt += 1
                    ah_mark[it[1]] = vcnt
                else:
                    vcnt += 1
                    eM_mark[(it[1], it[2])] = vcnt

            @block.gpsimd
            def _(g):
                g.wait_ge(ccs.sem, CC_M)
                if load_gate is not None:
                    g.wait_ge(load_gate[0], load_gate[1])
                for r in range(NCORES):
                    d_g.dma(g.dma_start(out=s_mh[:, r * D:(r + 1) * D],
                                        in_=cc_m_out[r * 256:r * 256 + 128, :]))
                    d_g.dma(g.dma_start(out=s_ml[:, r * D:(r + 1) * D],
                                        in_=cc_m_out[r * 256 + 128:r * 256 + 256, :]))
            G_MLOAD = d_g.n

            s_apply = []

            @block.scalar
            def _(s):
                for tt in range(NXT):
                    s.wait_ge(fV.sem, coef_mark)
                    if tt >= 2:
                        s.wait_ge(fT.sem, tr_mark[tt - 2])
                    i = s.activation(s_xlnf[:, (tt % 2) * D:(tt % 2 + 1) * D],
                                     sx[:, tt * D:(tt + 1) * D], AF.Identity,
                                     bias=s_stats[:, tt:tt + 1],
                                     scale=s_stats[:, 48 + tt:49 + tt])
                    s_apply.append(fS.inc(i, 1))

            @block.tensor
            def _(t):
                for it in order:
                    if it[0] == "tr":
                        tt = it[1]
                        t.wait_ge(fS.sem, s_apply[tt])
                        if tt == 0 and gate is not None:
                            t.wait_ge(gate[0], gate[1])
                        if tt >= 2:
                            t.wait_ge(fV.sem, eT_mark[tt - 2])
                        for c in range(8):
                            i = t.transpose(
                                pF[:, (tt % 2) * 1024 + c * 128:(tt % 2) * 1024 + (c + 1) * 128],
                                s_xlnf[:, (tt % 2) * D + c * 128:(tt % 2) * D + (c + 1) * 128],
                                s_id[:, :])
                        fT.inc(i, 1)
                    else:
                        tt, nh = it[1], it[2]
                        gidx = tt * 2 + nh
                        t.wait_ge(fV.sem, eT_mark[tt])
                        if gidx == 0:
                            t.wait_ge(d_g.sem, G_MLOAD)
                        if gidx >= 2:
                            t.wait_ge(fV.sem, eM_mark[((gidx - 2) // 2, (gidx - 2) % 2)])
                        om = pF[:, 2048 + (gidx % 2) * 512: 2048 + (gidx % 2 + 1) * 512]
                        for dc in range(8):
                            lhh = s_xlnTh[:, dc * TL + tt * 128: dc * TL + (tt + 1) * 128]
                            lhl = s_xlnTl[:, dc * TL + tt * 128: dc * TL + (tt + 1) * 128]
                            rhh = s_mh[:, dc * D + nh * 512: dc * D + (nh + 1) * 512]
                            rhl = s_ml[:, dc * D + nh * 512: dc * D + (nh + 1) * 512]
                            t.matmul(om, lhh, rhh, start=(dc == 0), stop=False)
                            noldw(t.matmul(om, lhh, rhl, start=False, stop=False))
                            i = t.matmul(om, lhl, rhh, start=False, stop=(dc == 7))
                        fT.inc(i, 1)

            @block.vector
            def _(v):
                for it in vorder:
                    if it[0] == "eT":
                        tt = it[1]
                        v.wait_ge(fT.sem, tr_mark[tt])
                        ps = pF[:, (tt % 2) * 1024:(tt % 2 + 1) * 1024]
                        dst_h = s_xlnTh[:, :].rearrange("p (dc t) -> p dc t", t=TL)[:, :, tt * 128:(tt + 1) * 128]
                        dst_l = s_xlnTl[:, :].rearrange("p (dc t) -> p dc t", t=TL)[:, :, tt * 128:(tt + 1) * 128]
                        src = ps.rearrange("p (dc t) -> p dc t", t=128)
                        i = v.tensor_copy(dst_h, src)
                        fV.inc(i, 1)
                        v.wait_ge(fV.sem, fV.n)
                        i = v.scalar_tensor_tensor(dst_l, src, 1.0, dst_h,
                                                   op0=ALU.mult, op1=ALU.subtract)
                        fV.inc(i, 1)
                    elif it[0] == "ah":
                        tt = it[1]
                        v.wait_ge(fS.sem, s_apply[tt])
                        i = v.tensor_add(sx[:, tt * D:(tt + 1) * D],
                                         sx[:, tt * D:(tt + 1) * D], s_fbhbc[:, :])
                        fV.inc(i, 1)
                    else:
                        tt, nh = it[1], it[2]
                        gidx = tt * 2 + nh
                        v.wait_ge(fT.sem, g_mark[(tt, nh)])
                        v.wait_ge(fV.sem, ah_mark[tt])
                        dst = sx[:, tt * D + nh * 512: tt * D + (nh + 1) * 512]
                        i = v.tensor_add(dst, dst,
                                         pF[:, 2048 + (gidx % 2) * 512: 2048 + (gidx % 2 + 1) * 512])
                        fV.inc(i, 1)

            if write_out:
                @block.sync
                def _(sync):
                    for tt in range(NXT):
                        sync.wait_ge(fV.sem, eM_mark[(tt, 1)])
                        d_out.dma(sync.dma_start(out=out_t[:, tt, :],
                                                 in_=sx[:, tt * D:(tt + 1) * D]))
            fctx.close()
            return eM_mark

        coef1 = emit_ln_stats(d_in.sem, [DX[tt // 4] for tt in range(NXT)])
        ffn1_done = emit_ffn("f1", coef1, write_out=(stage == 1),
                             gate=(vb.sem, VB_BIAS))
        if stage == 1:
            @block.gpsimd
            def _(g):
                g.wait_ge(d_out.sem, 16 * NXT)
            _apply_ldw_reuse(nc, RMM)
            return nc
        FFN1_ALL = ffn1_done[(NXT - 1, 1)]

        actx = ExitStack()
        s_qT = actx.enter_context(nc.sbuf_tensor("s_qT", [128, 8 * TL], BF16))
        s_ckvT = actx.enter_context(nc.sbuf_tensor("s_ckvT", [128, 8192], BF16))
        s_ckva = actx.enter_context(nc.sbuf_tensor("s_ckva", [128, NCH * 96], BF16))

        pctx = ExitStack()
        s_xlnTa = pctx.enter_context(nc.sbuf_tensor("s_xlnTa", [128, 8 * TL], BF16))
        s_wqdT = pctx.enter_context(nc.sbuf_tensor("s_wqdT", [128, 8 * QL], BF16))
        s_wqkT = pctx.enter_context(nc.sbuf_tensor("s_wqkT", [128, 4 * D], BF16))
        s_wkvdT = pctx.enter_context(nc.sbuf_tensor("s_wkvdT", [128, 8 * LAT], BF16))
        s_cqT = pctx.enter_context(nc.sbuf_tensor("s_cqT", [128, 4 * 512], BF16))
        s_ckvl = pctx.enter_context(nc.sbuf_tensor("s_ckvl", [LAT, TL], BF16))
        pP = pctx.enter_context(nc.psum_tensor("pP", [128, 2048], F32))
        pPb = pctx.enter_context(nc.psum_tensor("pPb", [128, 1024], BF16))

        coef2 = emit_ln_stats(fV.sem, [ffn1_done[(tt, 1)] for tt in range(NXT)])

        @block.gpsimd
        def _(g):
            g.wait_ge(fV.sem, FFN1_ALL)
            d_g.dma(g.dma_start(out=s_wqdT[:, :].rearrange("p (dc n) -> p dc n", n=QL),
                                in_=wqdT.rearrange("(dc p) n -> p dc n", p=128)))
            d_g.dma(g.dma_start(out=s_wqkT[:, :].rearrange("p (qc n) -> p qc n", n=D),
                                in_=wqkT.rearrange("(qc p) n -> p qc n", p=128)))
            d_g.dma(g.dma_start(out=s_wkvdT[:, :].rearrange("p (dc n) -> p dc n", n=LAT),
                                in_=wkvdT.rearrange("(dc p) n -> p dc n", p=128)))
        G_WATT = d_g.n

        s_apply2 = []

        @block.scalar
        def _(s):
            for tt in range(NXT):
                s.wait_ge(fV.sem, coef2)
                if tt >= 2:
                    s.wait_ge(pT.sem, tt - 1)
                i = s.activation(s_xlnf[:, (tt % 2) * D:(tt % 2 + 1) * D],
                                 sx[:, tt * D:(tt + 1) * D], AF.Identity,
                                 bias=s_stats[:, tt:tt + 1],
                                 scale=s_stats[:, 48 + tt:49 + tt])
                s_apply2.append(pS.inc(i, 1))
        PS_APPLY2 = pS.n

        @block.tensor
        def _(t):
            for tt in range(NXT):
                t.wait_ge(pS.sem, s_apply2[tt])
                if tt == 0:
                    t.wait_ge(fV.sem, FFN1_ALL)
                if tt >= 2:
                    t.wait_ge(pV.sem, tt - 1)
                for c in range(8):
                    i = t.transpose(
                        pP[:, (tt % 2) * 1024 + c * 128:(tt % 2) * 1024 + (c + 1) * 128],
                        s_xlnf[:, (tt % 2) * D + c * 128:(tt % 2) * D + (c + 1) * 128],
                        s_id[:, :])
                pT.inc(i, 1)
        PT_TRANSP = pT.n

        @block.vector
        def _(v):
            for tt in range(NXT):
                v.wait_ge(pT.sem, tt + 1)
                ps = pP[:, (tt % 2) * 1024:(tt % 2 + 1) * 1024]
                dst = s_xlnTa[:, :].rearrange("p (dc t) -> p dc t", t=TL)[:, :, tt * 128:(tt + 1) * 128]
                i = v.tensor_copy(dst, ps.rearrange("p (dc t) -> p dc t", t=128))
                pV.inc(i, 1)
        PV_EVA = pV.n

        def evac_mark(tq, pos):
            return PS_APPLY2 + tq * 13 + pos + 1

        @block.tensor
        def _(t):
            t.wait_ge(pV.sem, PV_EVA)
            t.wait_ge(d_g.sem, G_WATT)
            gi = 0
            for tq in range(4):
                if tq >= 1:
                    t.wait_ge(pS.sem, evac_mark(tq - 1, 0))
                for dc in range(8):
                    i = t.matmul(pP[0:64, 1024:1536],
                                 s_wkvdT[:, dc * LAT:(dc + 1) * LAT],
                                 s_xlnTa[:, dc * TL + tq * 512: dc * TL + (tq + 1) * 512],
                                 start=(dc == 0), stop=(dc == 7))
                pT.inc(i, 1)
                for qlc in range(4):
                    gi += 1
                    if gi > 2:
                        t.wait_ge(pS.sem, PS_APPLY2 + _gi_to_evac(gi - 2))
                    om = pP[:, ((gi % 2)) * 512:((gi % 2) + 1) * 512]
                    for dc in range(8):
                        i = t.matmul(om,
                                     s_wqdT[:, dc * QL + qlc * 128: dc * QL + qlc * 128 + 128],
                                     s_xlnTa[:, dc * TL + tq * 512: dc * TL + (tq + 1) * 512],
                                     start=(dc == 0), stop=(dc == 7))
                    pT.inc(i, 1)
                t.wait_ge(pS.sem, evac_mark(tq, 4))
                for hc in range(8):
                    gi += 1
                    if gi > 2:
                        t.wait_ge(pS.sem, PS_APPLY2 + _gi_to_evac(gi - 2))
                    om = pP[:, ((gi % 2)) * 512:((gi % 2) + 1) * 512]
                    for qlc in range(4):
                        i = t.matmul(om,
                                     s_wqkT[:, qlc * D + hc * 128: qlc * D + hc * 128 + 128],
                                     s_cqT[:, qlc * 512:(qlc + 1) * 512],
                                     start=(qlc == 0), stop=(qlc == 3))
                    pT.inc(i, 1)
        PT_QCHAIN = pT.n

        @block.scalar
        def _(s):
            tmark = PT_TRANSP
            for tq in range(4):
                tmark += 1
                s.wait_ge(pT.sem, tmark)
                i = s.activation(s_ckvl[:, tq * 512:(tq + 1) * 512],
                                 pP[0:64, 1024:1536], AF.Identity,
                                 bias=s_bcols[0:64, 12:13])
                pS.inc(i, 1)
                for qlc in range(4):
                    tmark += 1
                    s.wait_ge(pT.sem, tmark)
                    gi2 = tq * 12 + qlc + 1
                    om = pP[:, (gi2 % 2) * 512:(gi2 % 2 + 1) * 512]
                    i = s.activation(s_cqT[:, qlc * 512:(qlc + 1) * 512], om,
                                     AF.Identity, bias=s_bcols[:, qlc:qlc + 1])
                    pS.inc(i, 1)
                for hc in range(8):
                    tmark += 1
                    s.wait_ge(pT.sem, tmark)
                    gi2 = tq * 12 + 4 + hc + 1
                    om = pP[:, (gi2 % 2) * 512:(gi2 % 2 + 1) * 512]
                    i = s.activation(s_qT[:, hc * TL + tq * 512: hc * TL + (tq + 1) * 512],
                                     om, AF.Identity, scale=0.125,
                                     bias=s_bcols[:, 4 + hc:5 + hc])
                    pS.inc(i, 1)
        PS_PRE = pS.n

        @block.gpsimd
        def _(g):
            g.wait_ge(pS.sem, evac_mark(3, 0))
            d_g.dma(g.dma_start(out=cc_kv_in, in_=s_ckvl[:, :]))
            g.wait_ge(d_g.sem, d_g.n)
            i = g.collective_compute(
                "AllGather", ALU.bypass,
                replica_groups=[[0, 1, 2, 3], [4, 5, 6, 7]],
                ins=[cc_kv_in], outs=[cc_kv_out])
            ccs.inc(i, 1)
            g.wait_ge(ccs.sem, ccs.n)
            d_g.dma(g.dma_start(out=s_ckvT[0:64, :].rearrange("l (r t) -> l r t", t=TL),
                                in_=cc_kv_out.rearrange("(r l) t -> l r t", l=LAT)))
            d_g.dma(g.dma_start(out=s_ckvT[64:128, :].rearrange("l (r t) -> l r t", t=TL),
                                in_=cc_kv_out.rearrange("(r l) t -> l r t", l=LAT)))
        G_CKVT = d_g.n

        @block.tensor
        def _(t):
            t.wait_ge(d_g.sem, G_CKVT)
            for b4 in range(4):
                if b4 >= 1:
                    t.wait_ge(pV.sem, PV_EVA + b4)
                for i16 in range(16):
                    ch = b4 * 16 + i16
                    i = t.transpose(pPb[:, i16 * 64:(i16 + 1) * 64],
                                    s_ckvT[0:64, ch * 128:(ch + 1) * 128],
                                    s_idb[0:64, 0:64])
                pT.inc(i, 1)
        PT_KVT = pT.n

        @block.vector
        def _(v):
            ckva_v = s_ckva[:, :].rearrange("p (c w) -> p c w", w=96)
            for b4 in range(4):
                v.wait_ge(pT.sem, PT_KVT - 3 + b4)
                i = v.tensor_copy(ckva_v[:, b4 * 16:(b4 + 1) * 16, 0:64],
                                  pPb[:, :].rearrange("p (c w) -> p c w", w=64))
                pV.inc(i, 1)
            i = v.memset(ckva_v[:, :, 64:96], 1.0)
            pV.inc(i, 1)
        PV_KV = pV.n

        pctx.close()

        ATT = 128
        NAT = TL // ATT
        mctx = ExitStack()
        NPAR = 6
        s_attn = mctx.enter_context(nc.sbuf_tensor("s_attn", [128, NPAR * H * ATT], BF16))
        s_wvuT = mctx.enter_context(nc.sbuf_tensor("s_wvuT", [128, D], BF16))
        s_wprojT = mctx.enter_context(nc.sbuf_tensor("s_wprojT", [128, 8 * D], BF16))
        s_zn = mctx.enter_context(nc.sbuf_tensor("s_zn", [64, H * ATT], BF16))
        s_yT = mctx.enter_context(nc.sbuf_tensor("s_yT", [128, 8 * ATT], BF16))
        s_rsx = mctx.enter_context(nc.sbuf_tensor("s_rsx", [96, H * ATT], F32))
        s_bvubc = mctx.enter_context(nc.sbuf_tensor("s_bvubc", [128, 1024], F32))
        p_sc = mctx.enter_context(nc.psum_tensor("p_sc", [128, 2048], F32))
        p_zr = mctx.enter_context(nc.psum_tensor("p_zr", [128, 2048], F32))

        @block.gpsimd
        def _(g):
            g.wait_ge(pS.sem, PS_PRE)
            g.wait_ge(pT.sem, PT_QCHAIN)
            d_g.dma(g.dma_start(out=s_wvuT[:, :], in_=wvuT2))
            d_g.dma(g.dma_start(out=s_bvubc[:, :], in_=bvubc_d))
            d_g.dma(g.dma_start(out=s_wprojT[:, :].rearrange("p (cc n) -> p cc n", n=D),
                                in_=wprojT.rearrange("(cc p) n -> p cc n", p=128)))
        G_WMAIN = d_g.n

        mT_base = mT.n
        mV_base = mV.n

        def mTy(tt):
            return mT_base + tt * 3 + 1

        def mTpj(tt, nh):
            return mT_base + tt * 3 + 2 + nh

        def mVbp(tt):
            return mV_base + tt * 5 + 1

        def mVrec(tt):
            return mV_base + tt * 5 + 2

        def mVzn(tt):
            return mV_base + tt * 5 + 3

        def mVy(tt):
            return mV_base + tt * 5 + 4

        def mVpj(tt):
            return mV_base + tt * 5 + 5

        G_MAIN0 = d_g.n

        def gRS(tt):
            return G_MAIN0 + 16 * 2 * (tt + 1)

        def emit_zrs(t, tt, cz):
            g = tt * NCH + cz
            t.wait_ge(aE.sem, tt * 2 * NCH + (cz + 1) * 2)
            base = (g % NPAR) * 2048
            lst = s_ckva[:, cz * 96:(cz + 1) * 96]
            first = cz == 0
            last = cz == NCH - 1
            for bj in range(4):
                if bj == 2:
                    aZ.inc(i, 1)
                    if cz == 0 and tt > 0:
                        t.wait_ge(mV.sem, mVpj(tt - 1))
                i = t.matmul(p_zr[0:96, bj * 512:(bj + 1) * 512], lst,
                             s_attn[:, base + bj * 512:base + (bj + 1) * 512],
                             start=first, stop=last)
                if bj > 0:
                    noldw(i)
            aZ3.inc(i, 1)

        def emit_tail(t, tt):
            t.wait_ge(mV.sem, mVzn(tt))
            for m in range(NP):
                colA = (m // 4) * 1024 + (m % 4) * 128
                t.matmul(p_zr[0:64, m * ATT:(m + 1) * ATT],
                         s_wvuT[0:64, (2 * m) * 64:(2 * m + 1) * 64],
                         s_zn[0:64, colA:colA + ATT],
                         start=True, stop=True, tile_position=(0, 0))
                i = t.matmul(p_zr[64:128, m * ATT:(m + 1) * ATT],
                             s_wvuT[0:64, (2 * m + 1) * 64:(2 * m + 2) * 64],
                             s_zn[0:64, colA + 512:colA + 512 + ATT],
                             start=True, stop=True, tile_position=(0, 64))
            mT.inc(i, 1)
            t.wait_ge(mV.sem, mVy(tt))
            for nh in range(2):
                for cc in range(8):
                    i = t.matmul(p_zr[:, 1024 + nh * 512:1536 + nh * 512],
                                 s_yT[:, cc * ATT:(cc + 1) * ATT],
                                 s_wprojT[:, cc * D + nh * 512: cc * D + (nh + 1) * 512],
                                 start=(cc == 0), stop=(cc == 7))
                mT.inc(i, 1)

        TAIL_AT = 5

        @block.tensor
        def _(t):
            t.wait_ge(d_g.sem, G_WMAIN)
            t.wait_ge(pV.sem, PV_KV)
            qA = s_qT[0:64, :].rearrange("p (m t2) -> p m t2", t2=TL)
            qB = s_qT[64:128, :].rearrange("p (m t2) -> p m t2", t2=TL)
            for tt in range(NAT):
                base_half = tt * 2 * NCH
                tsl = slice(tt * ATT, (tt + 1) * ATT)
                pend = [] if tt > 0 else None
                for c in range(NCH):
                    if c == 0:
                        if tt > 0:
                            t.wait_ge(aE.sem, base_half)
                    else:
                        t.wait_ge(aE.sem, base_half + 2 * c - 1)
                    lA = s_ckvT[0:64, c * 128:(c + 1) * 128]
                    lB = s_ckvT[64:128, c * 128:(c + 1) * 128]
                    t.matmul(p_sc[:, 0:512], lA, qA[:, 0:4, tsl],
                             start=True, stop=True, tile_position=(0, 0))
                    i = t.matmul(p_sc[:, 1024:1536], lB, qB[:, 0:4, tsl],
                                 start=True, stop=True, tile_position=(64, 0))
                    aS.inc(i, 1)
                    if c >= 1:
                        t.wait_ge(aE.sem, base_half + 2 * c)
                    t.matmul(p_sc[:, 512:1024], lA, qA[:, 4:8, tsl],
                             start=True, stop=True, tile_position=(0, 0))
                    i = t.matmul(p_sc[:, 1536:2048], lB, qB[:, 4:8, tsl],
                                 start=True, stop=True, tile_position=(64, 0))
                    aS.inc(i, 1)
                    if pend is not None and c == TAIL_AT:
                        emit_tail(t, tt - 1)
                        for cz in pend:
                            emit_zrs(t, tt, cz)
                        pend = None
                    if c >= 1:
                        if pend is not None:
                            pend.append(c - 1)
                        else:
                            emit_zrs(t, tt, c - 1)
                emit_zrs(t, tt, NCH - 1)
            emit_tail(t, NAT - 1)

        SB_BASE = sb.n

        def emit_recipS(s, tt):
            s.wait_ge(aZ.sem, (tt + 1) * NCH)
            s.wait_ge(aZ3.sem, (tt + 1) * NCH)
            i = s.activation(s_xlnf[64:96, 0:1536], p_zr[64:96, 512:2048], AF.Ln)
            sb.inc(i, 1)
            s.wait_ge(sb.sem, sb.n)
            i = s.activation(s_rsx[64:96, 512:2048], s_xlnf[64:96, 0:1536],
                             AF.Exp, scale=-1.0)
            sb.inc(i, 1)

        @block.scalar
        def _(s):
            for tt in range(NAT):
                base_half = tt * 2 * NCH
                for c in range(NCH):
                    g = tt * NCH + c
                    cp = g % NPAR
                    if tt > 0 and c == 0:
                        emit_recipS(s, tt - 1)
                    for hh in range(2):
                        gidx = c * 2 + hh
                        s.wait_ge(aS.sem, base_half + gidx + 1)
                        if g >= NPAR:
                            if hh == 0:
                                s.wait_ge(aZ.sem, g - NPAR + 1)
                            else:
                                s.wait_ge(aZ3.sem, g - NPAR + 1)
                        slot = hh
                        src = p_sc[:, :].rearrange("p (a x) -> p a x", a=2)[:, :, slot * 512:(slot + 1) * 512]
                        dst = s_attn[:, cp * 2048 + hh * 1024: cp * 2048 + (hh + 1) * 1024]
                        i = s.activation(dst.rearrange("p (a x) -> p a x", a=2), src, AF.Exp)
                        aE.inc(i, 1)
            emit_recipS(s, NAT - 1)

        @block.vector
        def _(v):
            for tt in range(NAT):
                v.wait_ge(pS.sem, s_apply2[tt])
                i = v.tensor_add(sx[:, tt * D:(tt + 1) * D],
                                 sx[:, tt * D:(tt + 1) * D], s_bprojbc[:, :])
                mV.inc(i, 1)
                v.wait_ge(aZ.sem, (tt + 1) * NCH)
                i = v.reciprocal(s_rsx[64:96, 0:512], p_zr[64:96, 0:512])
                mV.inc(i, 1)
                v.wait_ge(d_g.sem, gRS(tt))
                i = v.tensor_mul(s_zn[:, :], p_zr[0:64, :], s_rsx[0:64, :])
                mV.inc(i, 1)
                v.wait_ge(mT.sem, mTy(tt))
                i = v.tensor_add(s_yT[:, :], p_zr[:, 0:1024], s_bvubc[:, :])
                mV.inc(i, 1)
                v.wait_ge(mT.sem, mTpj(tt, 1))
                v.wait_ge(mV.sem, mVbp(tt))
                dst = sx[:, tt * D:(tt + 1) * D]
                i = v.tensor_add(dst, dst, p_zr[:, 1024:2048])
                mV.inc(i, 1)

        @block.gpsimd
        def _(g):
            for tt in range(NAT):
                g.wait_ge(mV.sem, mVrec(tt))
                g.wait_ge(sb.sem, SB_BASE + 2 * (tt + 1))
                d_g.dma(g.dma_start(
                    out=rs_dram.rearrange("(o hb) t -> o (hb t)", o=1),
                    in_=s_rsx[64:65, :]))
                g.wait_ge(d_g.sem, d_g.n)
                d_g.dma(g.dma_start(
                    out=s_rsx[0:64, :],
                    in_=rs_dram.rearrange("(o hb) t -> o (hb t)", o=1)[0:1].partition_broadcast(64)))

        mctx.close()
        actx.close()

        if stage == 2:
            @block.sync
            def _(sync):
                for tt in range(NXT):
                    sync.wait_ge(mV.sem, mVpj(tt))
                    d_out.dma(sync.dma_start(out=out_t[:, tt, :],
                                             in_=sx[:, tt * D:(tt + 1) * D]))

            @block.gpsimd
            def _(g):
                g.wait_ge(d_out.sem, 16 * NXT)
            _apply_ldw_reuse(nc, RMM)
            return nc

        ATT_ALL = mVpj(NXT - 1)
        coef3 = emit_ln_stats(mV.sem, [mVpj(tt) for tt in range(NXT)])
        emit_ffn("f2", coef3, write_out=True,
                 gate=(mV.sem, ATT_ALL), load_gate=(mV.sem, ATT_ALL))

        @block.gpsimd
        def _(g):
            g.wait_ge(d_out.sem, 16 * NXT)

    _apply_ldw_reuse(nc, RMM)
    return nc


def _gi_to_evac(gi):
    tq = (gi - 1) // 12
    pos = (gi - 1) % 12
    return tq * 13 + 2 + pos


def _prep_inputs(inputs):
    cs1, cs2, ident, identb = _host_consts()
    x = np.asarray(inputs["L0_attn_x"], np.float32)
    x_flat = np.ascontiguousarray(x.reshape(16 * 1024, D))
    wqdT = np.ascontiguousarray(np.asarray(inputs["L0_attn_wq_d"], np.float32).T)
    wqkT = np.ascontiguousarray(np.asarray(inputs["L0_attn_w_qk"], np.float32).T)
    wkvdT = np.ascontiguousarray(np.asarray(inputs["L0_attn_wkv_d"], np.float32).T)
    wvuT = np.ascontiguousarray(np.asarray(inputs["L0_attn_wv_u"], np.float32).T)
    wvuT2 = np.ascontiguousarray(np.concatenate([wvuT, wvuT], axis=0))
    wprojT = np.ascontiguousarray(np.asarray(inputs["L0_attn_w_proj"], np.float32).T)
    bq = np.asarray(inputs["L0_attn_bq_d"], np.float32).reshape(4, 128)
    bqk = np.asarray(inputs["L0_attn_b_qk"], np.float32).reshape(8, 128)
    bkv = np.zeros((1, 128), np.float32)
    bkv[0, :64] = np.asarray(inputs["L0_attn_bkv_d"], np.float32)
    bvu = np.asarray(inputs["L0_attn_bv_u"], np.float32).reshape(8, 128)
    bstack = np.ascontiguousarray(np.concatenate([bq, bqk, bkv, bvu], axis=0))
    bvubc = np.ascontiguousarray(np.concatenate(
        [np.repeat(bvu[m][:, None], 128, axis=1) for m in range(8)], axis=1))
    bprow = np.ascontiguousarray(np.asarray(inputs["L0_attn_b_proj"], np.float32).reshape(1, D))
    fbrow = np.ascontiguousarray(np.asarray(inputs["L0_ffn_bias"], np.float32).reshape(1, D))
    wfr = np.ascontiguousarray(np.asarray(inputs["L0_ffn_w_freq_real"], np.float32))
    wfi = np.ascontiguousarray(np.asarray(inputs["L0_ffn_w_freq_imag"], np.float32))

    in_maps = []
    for c in range(NCORES):
        shard = np.ascontiguousarray(x_flat[c * TL:(c + 1) * TL])
        in_maps.append({
            "xs": shard, "wqdT": wqdT, "wqkT": wqkT, "wkvdT": wkvdT,
            "wvuT2": wvuT2, "wprojT": wprojT, "wfr": wfr, "wfi": wfi,
            "bvubc": bvubc,
            "cs1": cs1[c], "cs2": cs2, "bstack": bstack, "bprow": bprow,
            "fbrow": fbrow, "identd": ident, "identbd": identb,
        })
    return in_maps


def run(inputs, stage=3):
    key = ("nc", stage)
    if key not in _NC_CACHE:
        _NC_CACHE[key] = build_nc(stage)
    nc = _NC_CACHE[key]
    in_maps = _prep_inputs(inputs)
    res = run_bass_kernel_spmd(nc, in_maps, core_ids=list(range(NCORES)))
    outs = np.concatenate([np.asarray(res.results[c]["out"], np.float32)
                           for c in range(NCORES)], axis=0)
    return outs.reshape(2, 8192, D)


def kernel(**inputs):
    return run(inputs, stage=3)
